# version 01
# baseline (speedup 1.0000x reference)
import math
import numpy as np
import concourse.bass as bass
import concourse.mybir as mybir
from concourse.bass_utils import run_bass_kernel_spmd

F32 = mybir.dt.float32
BF16 = mybir.dt.bfloat16
AF = mybir.ActivationFunctionType
ALU = mybir.AluOpType
AX = mybir.AxisListType

H = 2048
NCH = 16
T = 2048
TF = 1024
E8 = 8
II = 1408
NIC = 11
EPS = 1e-5
ISCL = 1.0 / math.sqrt(2048.0)
BIG = 1.0e30

ENGS = ("sync", "gpsimd", "tensor", "vector", "scalar")


class Plan:
    def __init__(self):
        self.ops = {e: [] for e in ENGS}
        self.sems = {}
        self.floors = {e: {} for e in ENGS}

    def op(self, eng, fn, waits=(), inc=None, inc_by=1):
        wl = []
        for s, v in waits:
            if v is None or v <= 0:
                continue
            if self.floors[eng].get(s, 0) < v:
                self.floors[eng][s] = v
                wl.append((s, v))
        if inc is not None:
            self.sems[inc] = self.sems.get(inc, 0) + inc_by
        self.ops[eng].append((fn, wl, inc, inc_by))
        return self.sems.get(inc) if inc is not None else None

    def bump(self, s, n):
        self.sems[s] = self.sems.get(s, 0) + n
        return self.sems[s]

    def cnt(self, s):
        return self.sems.get(s, 0)


def _bf(a):
    import ml_dtypes
    return np.ascontiguousarray(a).astype(ml_dtypes.bfloat16)


def build_program(upto="full", dbg=False, world=8):
    nc = bass.Bass("TRN2", target_bir_lowering=False)

    x0_d = nc.dram_tensor("x0", [H, T], F32, kind="ExternalInput")
    wqkvT_d = nc.dram_tensor("wqkvT", [H, 3 * H], F32, kind="ExternalInput")
    bqkv_pc_d = nc.dram_tensor("bqkv_pc", [128, 48], F32, kind="ExternalInput")
    bqkv_row_d = nc.dram_tensor("bqkv_row", [1, 3 * H], F32, kind="ExternalInput")
    woutT_d = nc.dram_tensor("woutT", [H, H], F32, kind="ExternalInput")
    bout_pc_d = nc.dram_tensor("bout_pc", [128, 16], F32, kind="ExternalInput")
    gatewT_d = nc.dram_tensor("gatew_pc", [128, NCH * E8], F32, kind="ExternalInput")
    wg_d = nc.dram_tensor("wg", [E8, H, II], F32, kind="ExternalInput")
    wu_d = nc.dram_tensor("wu", [E8, H, II], F32, kind="ExternalInput")
    wd_d = nc.dram_tensor("wd", [E8, II, H], F32, kind="ExternalInput")
    swg_d = nc.dram_tensor("swg", [H, 4 * II], F32, kind="ExternalInput")
    swu_d = nc.dram_tensor("swu", [H, 4 * II], F32, kind="ExternalInput")
    swd_d = nc.dram_tensor("swd", [4 * II, H], F32, kind="ExternalInput")
    sgw_d = nc.dram_tensor("sgw_pc", [128, NCH], F32, kind="ExternalInput")

    y_d = nc.dram_tensor("y", [H, T], F32, kind="ExternalOutput")

    x1_d = nc.dram_tensor("x1buf", [H, T], F32, kind="Internal")
    x2_d = nc.dram_tensor("x2buf", [H, T], F32, kind="Internal")
    q_dram = nc.dram_tensor("qbuf", [H, T], BF16, kind="Internal")
    cc_in_d = nc.dram_tensor("cc_in", [2 * T, H], BF16, kind="Internal")
    cc_out_d = nc.dram_tensor("cc_out", [world * 2 * T, H], BF16, kind="Internal",
                              addr_space="Shared" if world == 8 else "Local")
    kvp_d = nc.dram_tensor("kvp", [2 * T, H], BF16, kind="Internal")

    ident_c = nc.inline_tensor(_bf(np.eye(128, dtype=np.float32)), name="ident_c")
    onescol_c = nc.inline_tensor(_bf(np.ones((128, 1), np.float32)), name="onescol_c")
    onesrow_c = nc.inline_tensor(_bf(np.ones((1, 128), np.float32)), name="onesrow_c")
    onesrowf_c = nc.inline_tensor(np.ones((128, 128), np.float32), name="onesrowf_c")
    oh = np.zeros((8, 8 * 128), np.float32)
    for e in range(8):
        oh[e, e * 128:(e + 1) * 128] = 0.5
    oh05_c = nc.inline_tensor(_bf(oh), name="oh05_c")

    import contextlib
    ctx = contextlib.ExitStack()
    sb = lambda n, s, d: ctx.enter_context(nc.sbuf_tensor(n, s, d))
    psa = lambda n, s, d: ctx.enter_context(nc.psum_tensor(n, s, d))
    mksem = lambda n: ctx.enter_context(nc.semaphore(n))

    XN = sb("XN", [128, NCH * T], BF16)
    ACC = sb("ACC", [128, NCH * TF], F32)
    WS = sb("WS", [128, 3 * 6144], BF16)
    HB = sb("HB", [128, NIC * TF], BF16)
    MI = sb("MI", [128, 4096], F32)
    CB = sb("CB", [128, 1024], BF16)
    OH05 = sb("OH05", [8, 8 * 128], BF16)
    IDENT = sb("IDENT", [128, 128], BF16)
    ONESC = sb("ONESC", [128, 1], BF16)
    ONESR = sb("ONESR", [1, 128], BF16)
    ONESRF = sb("ONESRF", [128, 128], F32)
    GWT = sb("GWT", [128, NCH * E8], BF16)
    SGWT = sb("SGWT", [128, NCH], BF16)
    BQKV = sb("BQKV", [128, 48], F32)
    BOUT = sb("BOUT", [128, 16], F32)

    P = [psa("P%d" % i, [128, 512], F32) for i in range(7)]
    PT = psa("PT", [128, 1024], BF16)

    SEMNAMES = ("pe", "ds", "as_", "cs", "kv",
                "w0", "w1", "w2", "gm0", "gm1", "qw0", "qw1",
                "sp0", "sp1", "k0", "k1", "v0", "v1", "q0",
                "xl0", "xl1", "as0", "as1", "ks0", "ks1",
                "vs0", "vs1", "xs0", "xs1")
    sems = {s: mksem(s) for s in SEMNAMES}

    pl = Plan()

    xnv = XN[:, :].rearrange("p (c t) -> p c t", t=T)
    accv = ACC[:, :].rearrange("p (c t) -> p c t", t=TF)
    ACCbf = ACC[:, :].bitcast(BF16)
    MIbf = MI[:, :].bitcast(BF16)
    wslot = [WS[:, i * 6144:(i + 1) * 6144] for i in range(3)]
    ST = MI[:, 0:1024]
    ST2 = MI[:, 1024:2048]
    A_t = MI[:, 2048:3072]
    B_t = MI[:, 3072:4096]
    DT0 = MI[:, 2048:2560]
    DT1 = MI[:, 2560:3072]
    DT2 = MI[:, 3072:3584]
    COL = [MI[:, 3584 + k:3584 + k + 1] for k in range(8)]
    SGBF = MIbf[0:1, 6144:7168]
    sgcv = lambda ic: XN[:, ic * T + TF: ic * T + 2 * TF]
    SQ_SCR = HB[:, 0:1024]
    SG_SCR = XN[:, 12 * T + TF: 12 * T + 2 * TF]
    UC_SCR = XN[:, 13 * T + TF: 13 * T + 2 * TF]
    C8_SCR = XN[:, 14 * T + TF: 14 * T + TF + 512]
    CTF = XN[0:8, 15 * T + TF: 15 * T + 2 * TF]
    CTM = MI[:, 2048:2176]

    def dma_raw(eng, out, in_, waits=(), inc=None):
        return pl.op(eng, lambda E_, o=out, i=in_: E_.dma_start(out=o, in_=i),
                     waits=waits, inc=inc, inc_by=16)

    class Stream:

        def __init__(self, eng, names):
            self.eng, self.names = eng, names
            self.i = 0

        def dma(self, out, in_, waits=()):
            s = self.names[self.i % len(self.names)]
            self.i += 1
            w = list(waits) + [(s, pl.cnt(s))]
            c = dma_raw(self.eng, out, in_, waits=w, inc=s)
            return (s, c)

        def all_waits(self):
            return [(s, pl.cnt(s)) for s in self.names]

    S_G = Stream("gpsimd", ["gm0", "gm1"])
    S_QW = Stream("gpsimd", ["qw0", "qw1"])
    S_SP = Stream("sync", ["sp0", "sp1"])
    S_K = Stream("sync", ["k0", "k1"])
    S_V = Stream("sync", ["v0", "v1"])
    S_Q = Stream("sync", ["q0"])
    S_XL = Stream("sync", ["xl0", "xl1"])
    S_AS = Stream("scalar", ["as0", "as1"])
    S_KS = Stream("scalar", ["ks0", "ks1"])
    S_VS = Stream("scalar", ["vs0", "vs1"])
    S_XS = Stream("scalar", ["xs0", "xs1"])

    def mmg(lro, waits=()):
        w = list(waits) + [("ds", pl.cnt("ds")), ("as_", pl.cnt("as_"))]

        def fn(E_, lro=tuple(lro)):
            r = None
            for (l, rh, o, st, sp) in lro:
                r = E_.matmul(o, l, rh, start=st, stop=sp)
            return r
        return pl.op("tensor", fn, waits=w, inc="pe")

    def tr(out, in_, waits=()):
        w = list(waits) + [("ds", pl.cnt("ds")), ("as_", pl.cnt("as_"))]
        p = in_.shape[0]
        return pl.op("tensor",
                     lambda E_, o=out, i=in_, p=p: E_.transpose(o, i, IDENT[0:p, 0:p]),
                     waits=w, inc="pe")

    def act(fn, waits=()):
        w = list(waits) + [("as_", pl.cnt("as_"))]
        return pl.op("scalar", fn, waits=w, inc="as_")

    def dve(fn, waits=()):
        w = list(waits) + [("ds", pl.cnt("ds"))]
        return pl.op("vector", fn, waits=w, inc="ds")

    def fmv(d):
        return d.ap().rearrange("(c p) t -> p c t", p=128)

    S_SP.dma(IDENT[:, :], ident_c.ap())
    S_SP.dma(ONESC[:, :], onescol_c.ap())
    S_SP.dma(ONESR[:, :], onesrow_c.ap())
    S_SP.dma(ONESRF[:, :], onesrowf_c.ap())
    S_SP.dma(OH05[:, :], oh05_c.ap())
    S_SP.dma(BQKV[:, :], bqkv_pc_d.ap())
    S_SP.dma(BOUT[:, :], bout_pc_d.ap())
    S_G.dma(GWT[:, :], gatewT_d.ap())
    S_G.dma(SGWT[:, :], sgw_d.ap())
    CONST_W = S_SP.all_waits()
    CONST_GW = S_G.all_waits()

    WQAHEAD = 2
    wq = {"pend": [], "issued": 0, "wof": {}, "done_pe": {}, "floor": 0}

    def wq_declare(src_ap, ncols):
        wq["pend"].append((src_ap, ncols))
        return len(wq["pend"]) - 1

    def wq_pump(upto_idx):
        while wq["issued"] <= min(upto_idx, len(wq["pend"]) - 1):
            j = wq["issued"]
            src, ncols = wq["pend"][j]
            w = [("pe", max(wq["done_pe"].get(j - 3, 0), wq["floor"]))]
            s = "w%d" % (j % 3)
            w.append((s, pl.cnt(s)))
            c = dma_raw("gpsimd", wslot[j % 3][:, 0:ncols], src, waits=w, inc=s)
            wq["wof"][j] = (s, c)
            wq["issued"] = j + 1

    def wq_use(j):
        wq_pump(j + WQAHEAD)
        assert j < wq["issued"]
        return wslot[j % 3], wq["wof"][j]

    def wq_done(j):
        wq["done_pe"][j] = pl.cnt("pe")

    def emit_ln(xd, tg0, xoff=0, dep=()):
        xv = fmv(xd)
        g0 = S_G.dma(xnv[:, :, xoff:xoff + TF], xv[:, :, tg0:tg0 + TF],
                     waits=list(dep) + [("pe", pl.cnt("pe")), ("ds", pl.cnt("ds"))])
        s0 = S_SP.dma(accv[:, :, :], xv[:, :, tg0:tg0 + TF],
                      waits=list(dep) + [("ds", pl.cnt("ds")), ("as_", pl.cnt("as_"))] +
                            S_AS.all_waits())
        pe_stats = None
        for hc in range(NCH):
            a_c = act(lambda E_, s=xnv[:, hc, xoff:xoff + TF], d=SQ_SCR:
                      E_.activation(d, s, AF.Square),
                      waits=[g0, ("pe", pl.cnt("pe"))])
            lro = []
            for b in range(2):
                lro.append((ONESC[:, :], xnv[:, hc, xoff + b * 512:xoff + (b + 1) * 512],
                            P[b][0:1, :], hc == 0, hc == NCH - 1))
            for b in range(2):
                lro.append((ONESC[:, :], SQ_SCR[:, b * 512:(b + 1) * 512],
                            P[2 + b][0:1, :], hc == 0, hc == NCH - 1))
            pe_stats = mmg(lro, waits=[g0, ("as_", a_c)] + CONST_W)
        d0 = None
        for b in range(2):
            dve(lambda E_, s=P[b][0:1, :], d=ST[0:1, b * 512:(b + 1) * 512]:
                E_.tensor_copy(d, s), waits=[("pe", pe_stats)])
            d0 = dve(lambda E_, s=P[2 + b][0:1, :], d=ST2[0:1, b * 512:(b + 1) * 512]:
                     E_.tensor_copy(d, s))
        inv = 1.0 / float(H)
        d1 = dve(lambda E_: E_.tensor_scalar(ST[0:1, :], ST[0:1, :], inv, None, ALU.mult),
                 waits=[("ds", d0)])
        d2 = dve(lambda E_: E_.tensor_tensor(A_t[0:1, :], ST[0:1, :], ST[0:1, :], ALU.mult),
                 waits=[("ds", d1)])
        d3 = dve(lambda E_: E_.tensor_scalar(ST2[0:1, :], ST2[0:1, :], inv, None, ALU.mult),
                 waits=[("ds", d2)])
        d4 = dve(lambda E_: E_.tensor_tensor(ST2[0:1, :], ST2[0:1, :], A_t[0:1, :],
                                             ALU.subtract),
                 waits=[("ds", d3)])
        d4b = dve(lambda E_: E_.tensor_scalar(ST2[0:1, :], ST2[0:1, :], EPS, None,
                                              ALU.add),
                  waits=[("ds", d4)])
        a1 = act(lambda E_: E_.activation(B_t[0:1, :], ST2[0:1, :], AF.Sqrt),
                 waits=[("ds", d4b)])
        d5 = dve(lambda E_: E_.reciprocal(A_t[0:1, :], B_t[0:1, :]), waits=[("as_", a1)])
        d6 = dve(lambda E_: E_.tensor_tensor(B_t[0:1, :], ST[0:1, :], A_t[0:1, :],
                                             ALU.mult),
                 waits=[("ds", d5)])
        lro = []
        for b in range(2):
            lro.append((ONESRF[0:1, :], A_t[0:1, b * 512:(b + 1) * 512], P[b][:, :],
                        True, True))
            lro.append((ONESRF[0:1, :], B_t[0:1, b * 512:(b + 1) * 512], P[2 + b][:, :],
                        True, True))
        peb = mmg(lro, waits=[("ds", d6)])
        dab = None
        for b in range(2):
            dve(lambda E_, s=P[b][:, :], d=A_t[:, b * 512:(b + 1) * 512]:
                E_.tensor_copy(d, s), waits=[("pe", peb)])
            dab = dve(lambda E_, s=P[2 + b][:, :], d=B_t[:, b * 512:(b + 1) * 512]:
                      E_.tensor_copy(d, s))
        APS = ST[:, 0:512]
        dlast = None
        for hc in range(NCH):
            for b in range(2):
                t0 = dve(lambda E_, xc=accv[:, hc, b * 512:(b + 1) * 512],
                         a=A_t[:, b * 512:(b + 1) * 512]:
                         E_.tensor_tensor(APS, xc, a, ALU.mult),
                         waits=[s0, ("ds", dab), ("pe", pe_stats)])
                dlast = dve(lambda E_, bb=B_t[:, b * 512:(b + 1) * 512],
                            d=xnv[:, hc, xoff + b * 512:xoff + (b + 1) * 512]:
                            E_.tensor_tensor(d, APS, bb, ALU.subtract),
                            waits=[("ds", t0)])
        return dlast

    def emit_gating(xn_ds):
        gwv = GWT[:, :].rearrange("p (c e) -> p c e", c=NCH)
        pe_l = None
        for tb in range(2):
            lro = []
            for hc in range(NCH):
                lro.append((gwv[:, hc, :], xnv[:, hc, tb * 512:(tb + 1) * 512],
                            P[4 + tb][0:8, :], hc == 0, hc == NCH - 1))
            pe_l = mmg(lro, waits=[("ds", xn_ds)] + CONST_GW)
        pe_s = None
        for tb, bank in ((0, P[6]), (1, P[3])):
            lro = []
            for hc in range(NCH):
                lro.append((SGWT[:, hc:hc + 1], xnv[:, hc, tb * 512:(tb + 1) * 512],
                            bank[0:1, :], hc == 0, hc == NCH - 1))
            pe_s = mmg(lro, waits=[("ds", xn_ds)] + CONST_GW)
        dl = None
        for tb, bank in ((0, P[4]), (1, P[5])):
            dl = dve(lambda E_, s=bank[0:8, :], d=CTF[0:8, tb * 512:(tb + 1) * 512]:
                     E_.tensor_copy(d, s), waits=[("pe", pe_l)])
        a_s = act(lambda E_: E_.activation(ST[0:1, 0:512], P[6][0:1, :], AF.Sigmoid),
                  waits=[("pe", pe_s)])
        a_s = act(lambda E_: E_.activation(ST[0:1, 512:1024], P[3][0:1, :], AF.Sigmoid))
        ds_s = dve(lambda E_: E_.tensor_scalar(SGBF[0:1, :], ST[0:1, :], 0.5, None,
                                               ALU.mult),
                   waits=[("as_", a_s)])
        dcm = None
        for tc in range(8):
            t_pe = tr(PT[:, tc * 8:(tc + 1) * 8], CTF[0:8, tc * 128:(tc + 1) * 128],
                      waits=[("ds", dl)])
            dcm = dve(lambda E_, s=PT[:, tc * 8:(tc + 1) * 8], d=CTM[:, tc * 8:(tc + 1) * 8]:
                      E_.tensor_copy(d, s), waits=[("pe", t_pe)])
        ds_prev = dcm
        for tc in range(8):
            sl = CTM[:, tc * 8:(tc + 1) * 8]
            st = CTM[:, 64 + tc * 8:64 + (tc + 1) * 8]
            d1 = dve(lambda E_, s=sl: E_.reduce_max(COL[0], s, axis=AX.X, negate=True),
                     waits=[("ds", ds_prev)])
            d6 = dve(lambda E_, s=sl, d=st: E_.tensor_scalar(d, s, COL[0], None, ALU.add),
                     waits=[("ds", d1)])
            d7 = dve(lambda E_, s=st, d=st: E_.tensor_scalar(d, s, 0.0, None, ALU.is_ge),
                     waits=[("ds", d6)])
            d8 = dve(lambda E_, s=st, d=st: E_.tensor_scalar(d, s, BIG, None, ALU.mult),
                     waits=[("ds", d7)])
            d9 = dve(lambda E_, a=sl, b=st, d=st: E_.tensor_tensor(d, a, b, ALU.subtract),
                     waits=[("ds", d8)])
            da = dve(lambda E_, s=st: E_.reduce_max(COL[1], s, axis=AX.X),
                     waits=[("ds", d9)])
            ab = act(lambda E_: E_.activation(COL[2], COL[1], AF.Exp, bias=COL[0]),
                     waits=[("ds", da)])
            db = dve(lambda E_: E_.tensor_scalar(COL[3], COL[2], 1.0, None, ALU.add),
                     waits=[("as_", ab)])
            dc = dve(lambda E_: E_.reciprocal(COL[4], COL[3]), waits=[("ds", db)])
            ac = act(lambda E_, s=sl, d=st: E_.activation(d, s, AF.Exp, bias=COL[0]),
                     waits=[("ds", dc)])
            dd = dve(lambda E_, s=sl, d=sl: E_.tensor_scalar(d, s, COL[1], None, ALU.is_ge),
                     waits=[("as_", ac)])
            de = dve(lambda E_, a=st, b=sl, d=st: E_.tensor_tensor(d, a, b, ALU.mult),
                     waits=[("ds", dd)])
            df = dve(lambda E_, s=st, d=C8_SCR[:, tc * 8:(tc + 1) * 8]:
                     E_.tensor_scalar(d, s, COL[4], None, ALU.mult),
                     waits=[("ds", de)])
            t_pe = tr(PT[0:8, 0:128], C8_SCR[:, tc * 8:(tc + 1) * 8], waits=[("ds", df)])
            ds_prev = dve(lambda E_, s=PT[0:8, 0:128],
                          d=CTF[0:8, tc * 128:(tc + 1) * 128]:
                          E_.tensor_copy(d, s), waits=[("pe", t_pe)])
        return ds_prev, ds_s

    def emit_cb(e, gate_ds):
        lro = []
        for tb in range(2):
            if e < E8:
                lro.append((OH05[:, e * 128:(e + 1) * 128],
                            CTF[0:8, tb * 512:(tb + 1) * 512], P[4 + tb][:, :], True, True))
            else:
                lro.append((ONESR[:, :], SGBF[0:1, tb * 512:(tb + 1) * 512],
                            P[4 + tb][:, :], True, True))
        peb = mmg(lro, waits=[("ds", gate_ds)] + CONST_W)
        dc = None
        for tb in range(2):
            dc = dve(lambda E_, s=P[4 + tb][:, :], d=CB[:, tb * 512:(tb + 1) * 512]:
                     E_.tensor_copy(d, s), waits=[("pe", peb)])
        return dc

    GU_Q = [(0, 384), (384, 384), (768, 384), (1152, 256)]
    D_Q = [(0, 512), (512, 512), (1024, 512), (1536, 512)]

    def declare_expert_pieces(e):
        if e < E8:
            wg_ap, wu_ap, wd_ap = wg_d.ap()[e], wu_d.ap()[e], wd_d.ap()[e]
        else:
            g = e - E8
            wg_ap = swg_d.ap()[:, g * II:(g + 1) * II]
            wu_ap = swu_d.ap()[:, g * II:(g + 1) * II]
            wd_ap = swd_d.ap()[g * II:(g + 1) * II, :]
        idx = {}
        for ph, wap, qs, cdim in (("g", wg_ap, GU_Q, NCH), ("u", wu_ap, GU_Q, NCH),
                                  ("d", wd_ap, D_Q, NIC)):
            for qi, (c0, cn) in enumerate(qs):
                src = wap.rearrange("(c p) i -> p c i", p=128)[:, :, c0:c0 + cn]
                idx[(ph, qi)] = (wq_declare(src, cdim * cn), c0, cn, cdim)
        return idx

    def emit_expert(e, xn_ds, cb_ds, pidx):
        def wv(ph, qi):
            j, c0, cn, cdim = pidx[(ph, qi)]
            slot, gw = wq_use(j)
            return (slot[:, 0:cdim * cn].rearrange("p (c i) -> p c i", c=cdim),
                    gw, j, c0)

        def gu_phase(ph, out_cb):
            last = None
            for ic in range(NIC):
                qi = min(ic * 128 // 384, 3)
                w, gw, j, c0 = wv(ph, qi)
                icol = ic * 128 - c0
                b0 = P[0] if ic % 2 == 0 else P[2]
                b1 = P[1] if ic % 2 == 0 else P[3]
                lro = []
                for hc in range(NCH):
                    l = w[:, hc, icol:icol + 128]
                    lro.append((l, xnv[:, hc, 0:512], b0[:, :], hc == 0, hc == NCH - 1))
                    lro.append((l, xnv[:, hc, 512:1024], b1[:, :], hc == 0, hc == NCH - 1))
                pe_c = mmg(lro, waits=[gw, ("ds", xn_ds)])
                if ic == NIC - 1 or min((ic + 1) * 128 // 384, 3) != qi:
                    wq_done(j)
                last = out_cb(ic, b0, b1, pe_c)
            return last

        def g_out(ic, b0, b1, pe_c):
            act(lambda E_, s=b0[:, :], d=sgcv(ic)[:, 0:512]:
                E_.activation(d, s, AF.Silu), waits=[("pe", pe_c)])
            return act(lambda E_, s=b1[:, :], d=sgcv(ic)[:, 512:1024]:
                       E_.activation(d, s, AF.Silu))

        g_as = gu_phase("g", g_out)

        def u_out(ic, b0, b1, pe_c):
            c0_ = dve(lambda E_, s=b0[:, :]: E_.tensor_copy(UC_SCR[:, 0:512], s),
                      waits=[("pe", pe_c)])
            c1_ = dve(lambda E_, s=b1[:, :]: E_.tensor_copy(UC_SCR[:, 512:1024], s))
            m1 = dve(lambda E_, ic=ic: E_.tensor_tensor(SG_SCR[:, :], sgcv(ic), CB[:, :],
                                                        ALU.mult),
                     waits=[("ds", c1_), ("as_", g_as), ("ds", cb_ds)])
            return dve(lambda E_, ic=ic: E_.tensor_tensor(HB[:, ic * TF:(ic + 1) * TF],
                                                          SG_SCR[:, :], UC_SCR[:, :],
                                                          ALU.mult),
                       waits=[("ds", m1)])

        u_ds = gu_phase("u", u_out)

        last_acc = None
        for hc2 in range(NCH):
            qi = hc2 * 128 // 512
            w, gw, j, c0 = wv("d", qi)
            hcol = hc2 * 128 - c0
            b0 = P[4] if hc2 % 2 == 0 else P[6]
            b1 = P[5] if hc2 % 2 == 0 else P[0]
            lro = []
            for ic in range(NIC):
                l = w[:, ic, hcol:hcol + 128]
                lro.append((l, HB[:, ic * TF:ic * TF + 512], b0[:, :],
                            ic == 0, ic == NIC - 1))
                lro.append((l, HB[:, ic * TF + 512:(ic + 1) * TF], b1[:, :],
                            ic == 0, ic == NIC - 1))
            pe_c = mmg(lro, waits=[gw, ("ds", u_ds)])
            if hc2 == NCH - 1 or (hc2 + 1) * 128 >= c0 + 512:
                wq_done(j)
            a0 = act(lambda E_, s=b0[:, :]: E_.copy(DT0, s), waits=[("pe", pe_c)])
            a1 = act(lambda E_, s=b1[:, :]: E_.copy(DT1, s))
            dve(lambda E_, hc2=hc2: E_.tensor_tensor(accv[:, hc2, 0:512],
                                                     accv[:, hc2, 0:512], DT0, ALU.add),
                waits=[("as_", a0)])
            last_acc = dve(lambda E_, hc2=hc2: E_.tensor_tensor(accv[:, hc2, 512:1024],
                                                                accv[:, hc2, 512:1024],
                                                                DT1, ALU.add),
                           waits=[("as_", a1)])
        return last_acc

    def emit_ffn_stage(xd, od, dep=()):
        s1 = None
        for hf in range(2):
            tg0 = hf * TF
            xn_ds = emit_ln(xd, tg0, 0, dep)
            gate_ds, sg_ds = emit_gating(xn_ds)
            pidx = [declare_expert_pieces(e) for e in range(12)]
            last = None
            for e in range(12):
                cb_ds = emit_cb(e, gate_ds if e < E8 else sg_ds)
                last = emit_expert(e, xn_ds, cb_ds, pidx[e])
            s1 = S_AS.dma(fmv(od)[:, :, tg0:tg0 + TF], accv[:, :, :],
                          waits=[("ds", last)])
        return s1

    def emit_attention(x1_dep):
        emit_ln(x1_d, 0, 0, x1_dep)
        xn2_ds = emit_ln(x1_d, TF, TF, x1_dep)

        BV = HB[:, 2048:4096]
        g_bv = S_G.dma(BV[0:1, :], bqkv_row_d.ap()[:, 2 * H:3 * H],
                       waits=[("ds", pl.cnt("ds"))])
        lro = [(ONESR[:, :], BV[0:1, b * 512:(b + 1) * 512], P[b][:, :], True, True)
               for b in range(4)]
        pebv = mmg(lro, waits=[g_bv] + CONST_W)
        dbv = None
        for b in range(4):
            dbv = dve(lambda E_, b=b, s=P[b][:, :]:
                      E_.tensor_copy(BV[:, b * 512:(b + 1) * 512], s),
                      waits=[("pe", pebv)])

        wqv = wqkvT_d.ap().rearrange("(c p) o -> p c o", p=128)
        slotA_flat = ACCbf[:, 0:16384]
        slotB_flat = WS[:, 0:16384]
        QT = HB[:, 0:2048]
        qkv_done_pe = {}
        qt_store = ("cs", 0)
        vt_store = {0: ("cs", 0), 1: ("cs", 0)}
        for pi in range(6):
            flat = slotB_flat if pi >= 4 else (slotA_flat if pi % 2 == 0 else slotB_flat)
            w = [("pe", qkv_done_pe.get(pi - 2, 0)), ("ds", xn2_ds),
                 ("as_", pl.cnt("as_")), ("ds", pl.cnt("ds"))]
            if pi == 5:
                w.append(("pe", qkv_done_pe.get(4, 0)))
            if pi < 2:
                w.append(("pe", pl.cnt("pe")))
            gq_w = S_QW.dma(flat, wqv[:, :, pi * 1024:(pi + 1) * 1024], waits=w)
            slot = flat.rearrange("p (c o) -> p c o", c=NCH)
            if pi < 4:
                for oi in range(8):
                    oc = pi * 8 + oi
                    pe_c = None
                    for tb in range(4):
                        lro = []
                        for hc in range(NCH):
                            lro.append((slot[:, hc, oi * 128:(oi + 1) * 128],
                                        xnv[:, hc, tb * 512:(tb + 1) * 512],
                                        P[tb][:, :], hc == 0, hc == NCH - 1))
                        pe_c = mmg(lro, waits=[gq_w, ("ds", xn2_ds)])
                    alast = None
                    for tb in range(4):
                        alast = act(lambda E_, tb=tb, oc=oc, s=P[tb][:, :]:
                                    E_.activation(QT[:, tb * 512:(tb + 1) * 512], s,
                                                  AF.Identity, bias=BQKV[:, oc:oc + 1]),
                                    waits=[("pe", pe_c), qt_store] + CONST_W)
                    dst = q_dram if oc < 16 else cc_in_d
                    row0 = (oc % 16) * 128
                    qt_store = S_KS.dma(dst.ap()[row0:row0 + 128, :], QT[:, :],
                                        waits=[("as_", alast)])
                qkv_done_pe[pi] = pl.cnt("pe")
            else:
                VT = HB[:, 4096 + (pi - 4) * 1024: 4096 + (pi - 3) * 1024]
                for tc in range(NCH):
                    pe_c = None
                    for ob in range(2):
                        lro = []
                        for hc in range(NCH):
                            lro.append((xnv[:, hc, tc * 128:(tc + 1) * 128],
                                        slot[:, hc, ob * 512:(ob + 1) * 512],
                                        P[4 + ob][:, :], hc == 0, hc == NCH - 1))
                        pe_c = mmg(lro, waits=[gq_w, ("ds", xn2_ds)])
                    dl = None
                    for ob in range(2):
                        c0_ = dve(lambda E_, ob=ob, s=P[4 + ob][:, :], vt=VT:
                                  E_.tensor_copy(vt[:, ob * 512:(ob + 1) * 512], s),
                                  waits=[("pe", pe_c), vt_store[pi - 4]])
                        dl = dve(lambda E_, ob=ob, pi=pi, vt=VT:
                                 E_.tensor_tensor(vt[:, ob * 512:(ob + 1) * 512],
                                                  vt[:, ob * 512:(ob + 1) * 512],
                                                  BV[:, (pi - 4) * 1024 + ob * 512:
                                                     (pi - 4) * 1024 + (ob + 1) * 512],
                                                  ALU.add),
                                 waits=[("ds", c0_), ("ds", dbv)])
                    vt_store[pi - 4] = S_VS.dma(
                        cc_in_d.ap()[T + tc * 128:T + (tc + 1) * 128,
                                     (pi - 4) * 1024:(pi - 3) * 1024],
                        VT[:, :], waits=[("ds", dl)])
                qkv_done_pe[pi] = pl.cnt("pe")

        kv_waits = S_KS.all_waits() + S_VS.all_waits()
        cs1 = pl.op("gpsimd",
                    lambda E_: E_.collective_compute(
                        "AllGather", ALU.bypass,
                        replica_groups=[list(range(world))],
                        ins=[cc_in_d.ap()], outs=[cc_out_d.ap()]),
                    waits=kv_waits, inc="cs", inc_by=1)

        def sw_fn(E_):
            pid = nc.partition_id([mybir.EngineType.Pool])
            for i in nc.Switch(engines=[E_], index=[pid], n=8):
                p = (i ^ 1) % world
                E_.dma_start(out=kvp_d.ap(),
                             in_=cc_out_d.ap()[p * 2 * T:(p + 1) * 2 * T, :]
                             ).then_inc(sems["kv"], 16)
        pl.op("gpsimd", sw_fn, waits=[("cs", pl.cnt("cs"))])
        kvp_w = ("kv", pl.bump("kv", 16))

        QTILE = ACCbf[:, 0:8192].rearrange("p (c t) -> p c t", c=NCH)
        SCq = [ACCbf[:, 8192 + q * 4096:8192 + (q + 1) * 4096] for q in range(4)]
        X1C = [ACC[:, 12288 + i * 512:12288 + (i + 1) * 512] for i in range(2)]
        PR = XN[:, 0:4096]
        PRT = XN[:, 4096:20480].rearrange("p (k t) -> p k t", k=32)
        CTX = XN[:, 20480:28672].rearrange("p (c t) -> p c t", c=NCH)
        KT = [WS[:, 0:8192], WS[:, 8192:16384]]
        VT2 = [HB[:, 4096 + i * 1024:4096 + (i + 1) * 1024] for i in range(2)]

        qv = q_dram.ap().rearrange("(c p) t -> p c t", p=128)
        kv_own = cc_in_d.ap()[0:T, :].rearrange("(c p) t -> p c t", p=128)
        kv_par = kvp_d.ap()[0:T, :].rearrange("(c p) t -> p c t", p=128)
        x1v = fmv(x1_d)
        x2v = fmv(x2_d)
        x2_store = {}
        last_x2 = None

        for qb in range(4):
            gq_q = S_Q.dma(ACCbf[:, 0:8192], qv[:, :, qb * 512:(qb + 1) * 512],
                           waits=[("pe", pl.cnt("pe")), ("ds", pl.cnt("ds"))] +
                                 (S_KS.all_waits() if qb == 0 else []))
            for kb in range(8):
                khalf = kb // 4
                src = kv_own if khalf == 0 else kv_par
                kw = [("pe", pl.cnt("pe"))]
                if khalf == 1:
                    kw.append(kvp_w)
                else:
                    kw += S_KS.all_waits() + S_VS.all_waits()
                gk = S_K.dma(KT[kb % 2],
                             src[:, :, (kb % 4) * 512:((kb % 4) + 1) * 512], waits=kw)
                ktv = KT[kb % 2].rearrange("p (c t) -> p c t", c=NCH)
                for qt in range(4):
                    bank = P[qt] if kb % 2 == 0 else P[(qt + 4) % 7]
                    lro = []
                    for hc in range(NCH):
                        lro.append((QTILE[:, hc, qt * 128:(qt + 1) * 128],
                                    ktv[:, hc, :], bank[:, :], hc == 0, hc == NCH - 1))
                    pe_c = mmg(lro, waits=[gk, gq_q])
                    dve(lambda E_, s=bank[:, :], qt=qt, kb=kb:
                        E_.tensor_copy(SCq[qt][:, kb * 512:(kb + 1) * 512], s),
                        waits=[("pe", pe_c)])
            for qt in range(4):
                d1 = dve(lambda E_, qt=qt: E_.reduce_max(COL[5], SCq[qt], axis=AX.X,
                                                         negate=True))
                d2 = dve(lambda E_: E_.tensor_scalar(COL[6], COL[5], ISCL, None, ALU.mult),
                         waits=[("ds", d1)])
                a1 = act(lambda E_, qt=qt: E_.activation(PR[:, :], SCq[qt], AF.Exp,
                                                         bias=COL[6], scale=ISCL,
                                                         accum_out=COL[7]),
                         waits=[("ds", d2), ("pe", pl.cnt("pe"))])
                d3 = dve(lambda E_: E_.reciprocal(COL[5], COL[7]), waits=[("as_", a1)])
                d4 = dve(lambda E_: E_.tensor_scalar(PR[:, :], PR[:, :], COL[5], None,
                                                     ALU.mult),
                         waits=[("ds", d3)])
                for kg in range(4):
                    pe_t = None
                    for kk in range(8):
                        ktc = kg * 8 + kk
                        pe_t = tr(PT[:, kk * 128:(kk + 1) * 128],
                                  PR[:, ktc * 128:(ktc + 1) * 128],
                                  waits=[("ds", d4)])
                    dve(lambda E_, kg=kg, qt=qt,
                        s=PT[:, :].rearrange("p (k t) -> p k t", k=8):
                        E_.tensor_copy(PRT[:, kg * 8:(kg + 1) * 8,
                                           qt * 128:(qt + 1) * 128], s),
                        waits=[("pe", pe_t)])
            prt_ds = pl.cnt("ds")
            ctx_as = None
            for gg, (h0, hn) in enumerate(((0, 7), (7, 7), (14, 2))):
                pe_c = None
                for ktc in range(32):
                    khalf = ktc // 16
                    src_d = cc_in_d if khalf == 0 else kvp_d
                    r0 = T + (ktc % 16) * 128
                    vw = [("pe", pl.cnt("pe"))]
                    if khalf == 1:
                        vw.append(kvp_w)
                    else:
                        vw += S_VS.all_waits()
                    gv = S_V.dma(VT2[ktc % 2][:, 0:hn * 128],
                                 src_d.ap()[r0:r0 + 128, h0 * 128:(h0 + hn) * 128],
                                 waits=vw)
                    lro = []
                    for hh in range(hn):
                        lro.append((VT2[ktc % 2][:, hh * 128:(hh + 1) * 128],
                                    PRT[:, ktc, :], P[hh][:, :], ktc == 0, ktc == 31))
                    pe_c = mmg(lro, waits=[gv, ("ds", prt_ds)])
                for hh in range(hn):
                    ctx_as = act(lambda E_, hh=hh, h0=h0, s=P[hh][:, :]:
                                 E_.copy(CTX[:, h0 + hh, :], s),
                                 waits=[("pe", pe_c)])
            for pi in range(4):
                gw_w = S_QW.dma(KT[pi % 2][:, 0:8192],
                                woutT_d.ap().rearrange("(c p) o -> p c o", p=128)
                                [:, :, pi * 512:(pi + 1) * 512],
                                waits=[("pe", pl.cnt("pe")), ("ds", pl.cnt("ds")),
                                       ("as_", pl.cnt("as_"))])
                wot = KT[pi % 2].rearrange("p (c o) -> p c o", c=NCH)
                for oi in range(4):
                    oc = pi * 4 + oi
                    bank = P[oc % 2]
                    lro = []
                    for hc in range(NCH):
                        lro.append((wot[:, hc, oi * 128:(oi + 1) * 128],
                                    CTX[:, hc, :], bank[:, :], hc == 0, hc == NCH - 1))
                    pe_c = mmg(lro, waits=[gw_w, ("as_", ctx_as)])
                    ao = act(lambda E_, s=bank[:, :], oc=oc:
                             E_.activation(DT2, s, AF.Identity, bias=BOUT[:, oc:oc + 1]),
                             waits=[("pe", pe_c)] + CONST_W)
                    gx = S_XL.dma(X1C[oc % 2], x1v[:, oc, qb * 512:(qb + 1) * 512],
                                  waits=[("ds", pl.cnt("ds")),
                                         x2_store.get(oc % 2, ("cs", 0))])
                    dx = dve(lambda E_, oc=oc: E_.tensor_tensor(X1C[oc % 2], X1C[oc % 2],
                                                                DT2, ALU.add),
                             waits=[("as_", ao), gx])
                    sx = S_XS.dma(x2v[:, oc, qb * 512:(qb + 1) * 512],
                                  X1C[oc % 2], waits=[("ds", dx)])
                    x2_store[oc % 2] = sx
                    last_x2 = sx
        return last_x2

    if upto == "ffn1":
        emit_ffn_stage(x0_d, y_d)
    elif upto == "attn":
        emit_ffn_stage(x0_d, x1_d)
        x1w = S_AS.all_waits()
        emit_attention(x1w)
        S_AS.dma(y_d.ap(), x2_d.ap(), waits=S_XS.all_waits())
    else:
        emit_ffn_stage(x0_d, x1_d)
        x1w = S_AS.all_waits()
        emit_attention(x1w)
        wq["floor"] = pl.cnt("pe")
        emit_ffn_stage(x2_d, y_d, dep=S_XS.all_waits())

    if dbg:
        if upto != "ffn1":
            dbg_cco = nc.dram_tensor("dbg_cco", [world * 2 * T, H], BF16,
                                     kind="ExternalOutput")
            dbg_kvp = nc.dram_tensor("dbg_kvp", [2 * T, H], BF16, kind="ExternalOutput")
            dbg_q = nc.dram_tensor("dbg_q", [H, T], BF16, kind="ExternalOutput")
            dbg_cci = nc.dram_tensor("dbg_cci", [2 * T, H], BF16, kind="ExternalOutput")
            dwx = [("pe", pl.cnt("pe")), ("ds", pl.cnt("ds")), ("as_", pl.cnt("as_"))]
            S_AS.dma(dbg_cco.ap(), cc_out_d.ap(), waits=dwx)
            S_AS.dma(dbg_kvp.ap(), kvp_d.ap(), waits=dwx)
            S_AS.dma(dbg_q.ap(), q_dram.ap(), waits=dwx)
            S_AS.dma(dbg_cci.ap(), cc_in_d.ap(), waits=dwx)
        dbg_xn = nc.dram_tensor("dbg_xn", [128, NCH * TF], BF16, kind="ExternalOutput")
        dbg_cb = nc.dram_tensor("dbg_cb", [128, 1024], BF16, kind="ExternalOutput")
        dbg_hb = nc.dram_tensor("dbg_hb", [128, NIC * TF], BF16, kind="ExternalOutput")

        dw = [("pe", pl.cnt("pe")), ("ds", pl.cnt("ds")), ("as_", pl.cnt("as_"))]
        S_AS.dma(dbg_xn.ap().rearrange("p (c t) -> p c t", t=TF), xnv[:, :, 0:TF],
                 waits=dw)
        S_AS.dma(dbg_cb.ap(), CB[:, :], waits=dw)
        S_AS.dma(dbg_hb.ap(), HB[:, :], waits=dw)


    fin = S_AS.all_waits() + S_XS.all_waits()
    pl.op("scalar", lambda E_: E_.nop(), waits=fin)

    with nc.Block() as block:
        def mk(name):
            def _(E_):
                for fn, wl, inc, inc_by in pl.ops[name]:
                    for s, v in wl:
                        E_.wait_ge(sems[s], v)
                    r = fn(E_)
                    if inc is not None and r is not None:
                        r.then_inc(sems[inc], inc_by)
            return _
        block.sync(mk("sync"))
        block.gpsimd(mk("gpsimd"))
        block.tensor(mk("tensor"))
        block.vector(mk("vector"))
        block.scalar(mk("scalar"))
    ctx.close()
    return nc


_NC_CACHE = {}


def kernel(L0_attn_hidden_states, L0_attn_w_qkv, L0_attn_b_qkv, L0_attn_w_out,
           L0_attn_b_out, L0_ffn_gate_w, L0_ffn_w_gate, L0_ffn_w_up, L0_ffn_w_down,
           L0_ffn_sw_gate, L0_ffn_sw_up, L0_ffn_sw_down, L0_ffn_shared_gate_w):
    x0 = np.asarray(L0_attn_hidden_states, dtype=np.float32)
    w_qkv = np.asarray(L0_attn_w_qkv, np.float32)
    b_qkv = np.asarray(L0_attn_b_qkv, np.float32)
    w_out = np.asarray(L0_attn_w_out, np.float32)
    b_out = np.asarray(L0_attn_b_out, np.float32)
    gate_w = np.asarray(L0_ffn_gate_w, np.float32)

    shared = {
        "wqkvT": np.ascontiguousarray(w_qkv.T),
        "bqkv_pc": np.ascontiguousarray(b_qkv.reshape(48, 128).T),
        "bqkv_row": np.ascontiguousarray(b_qkv.reshape(1, 3 * H)),
        "woutT": np.ascontiguousarray(w_out.T),
        "bout_pc": np.ascontiguousarray(b_out.reshape(16, 128).T),
        "gatew_pc": np.ascontiguousarray(
            gate_w.T.reshape(16, 128, 8).transpose(1, 0, 2).reshape(128, 128)),
        "sgw_pc": np.ascontiguousarray(
            np.asarray(L0_ffn_shared_gate_w, np.float32).reshape(16, 128).T),
        "wg": np.ascontiguousarray(np.asarray(L0_ffn_w_gate, np.float32)),
        "wu": np.ascontiguousarray(np.asarray(L0_ffn_w_up, np.float32)),
        "wd": np.ascontiguousarray(np.asarray(L0_ffn_w_down, np.float32)),
        "swg": np.ascontiguousarray(np.asarray(L0_ffn_sw_gate, np.float32)),
        "swu": np.ascontiguousarray(np.asarray(L0_ffn_sw_up, np.float32)),
        "swd": np.ascontiguousarray(np.asarray(L0_ffn_sw_down, np.float32)),
    }
    in_maps = []
    for c in range(8):
        b, hf = c // 2, c % 2
        m = dict(shared)
        m["x0"] = np.ascontiguousarray(x0[b, hf * T:(hf + 1) * T, :].T)
        in_maps.append(m)

    if "nc" not in _NC_CACHE:
        _NC_CACHE["nc"] = build_program("full")
    res = run_bass_kernel_spmd(_NC_CACHE["nc"], in_maps, core_ids=list(range(8)))
    out = np.empty((4, 2 * T, H), np.float32)
    for c in range(8):
        b, hf = c // 2, c % 2
        out[b, hf * T:(hf + 1) * T, :] = np.asarray(res.results[c]["y"]).T
    return out


# version 02
# speedup vs baseline: 1.4632x; 1.4632x over previous
import math
import numpy as np
import concourse.bass as bass
import concourse.mybir as mybir
from concourse.bass_utils import run_bass_kernel_spmd

F32 = mybir.dt.float32
BF16 = mybir.dt.bfloat16
AF = mybir.ActivationFunctionType
ALU = mybir.AluOpType
AX = mybir.AxisListType

H = 2048
NCH = 16
T = 2048
TF = 1024
E8 = 8
II = 1408
NIC = 11
EPS = 1e-5
ISCL = 1.0 / math.sqrt(2048.0)
BIG = 1.0e30

ENGS = ("sync", "gpsimd", "tensor", "vector", "scalar")


class Plan:
    def __init__(self):
        self.ops = {e: [] for e in ENGS}
        self.sems = {}
        self.floors = {e: {} for e in ENGS}

    def op(self, eng, fn, waits=(), inc=None, inc_by=1):
        wl = []
        for s, v in waits:
            if v is None or v <= 0:
                continue
            if self.floors[eng].get(s, 0) < v:
                self.floors[eng][s] = v
                wl.append((s, v))
        if inc is not None:
            self.sems[inc] = self.sems.get(inc, 0) + inc_by
        self.ops[eng].append((fn, wl, inc, inc_by))
        return self.sems.get(inc) if inc is not None else None

    def bump(self, s, n):
        self.sems[s] = self.sems.get(s, 0) + n
        return self.sems[s]

    def cnt(self, s):
        return self.sems.get(s, 0)


def _bf(a):
    import ml_dtypes
    return np.ascontiguousarray(a).astype(ml_dtypes.bfloat16)


def build_program(upto="full", dbg=False, world=8):
    nc = bass.Bass("TRN2", target_bir_lowering=False)

    x0_d = nc.dram_tensor("x0", [H, T], F32, kind="ExternalInput")
    wqkvT_d = nc.dram_tensor("wqkvT", [H, 3 * H], F32, kind="ExternalInput")
    bqkv_pc_d = nc.dram_tensor("bqkv_pc", [128, 48], F32, kind="ExternalInput")
    bqkv_row_d = nc.dram_tensor("bqkv_row", [1, 3 * H], F32, kind="ExternalInput")
    woutT_d = nc.dram_tensor("woutT", [H, H], F32, kind="ExternalInput")
    bout_pc_d = nc.dram_tensor("bout_pc", [128, 16], F32, kind="ExternalInput")
    gatewT_d = nc.dram_tensor("gatew_pc", [128, NCH * E8], F32, kind="ExternalInput")
    wg_d = nc.dram_tensor("wg", [E8, H, II], F32, kind="ExternalInput")
    wu_d = nc.dram_tensor("wu", [E8, H, II], F32, kind="ExternalInput")
    wd_d = nc.dram_tensor("wd", [E8, II, H], F32, kind="ExternalInput")
    swg_d = nc.dram_tensor("swg", [H, 4 * II], F32, kind="ExternalInput")
    swu_d = nc.dram_tensor("swu", [H, 4 * II], F32, kind="ExternalInput")
    swd_d = nc.dram_tensor("swd", [4 * II, H], F32, kind="ExternalInput")
    sgw_d = nc.dram_tensor("sgw_pc", [128, NCH], F32, kind="ExternalInput")

    y_d = nc.dram_tensor("y", [H, T], F32, kind="ExternalOutput")

    x1_d = nc.dram_tensor("x1buf", [H, T], F32, kind="Internal")
    x2_d = nc.dram_tensor("x2buf", [H, T], F32, kind="Internal")
    q_dram = nc.dram_tensor("qbuf", [H, T], BF16, kind="Internal")
    cc_in_d = nc.dram_tensor("cc_in", [2 * T, H], BF16, kind="Internal")
    cc_out_d = nc.dram_tensor("cc_out", [world * 2 * T, H], BF16, kind="Internal",
                              addr_space="Shared" if world == 8 else "Local")
    kvp_d = nc.dram_tensor("kvp", [2 * T, H], BF16, kind="Internal")

    ident_c = nc.inline_tensor(_bf(np.eye(128, dtype=np.float32)), name="ident_c")
    onescol_c = nc.inline_tensor(_bf(np.ones((128, 1), np.float32)), name="onescol_c")
    onesrow_c = nc.inline_tensor(_bf(np.ones((1, 128), np.float32)), name="onesrow_c")
    onesrowf_c = nc.inline_tensor(np.ones((128, 128), np.float32), name="onesrowf_c")
    oh = np.zeros((8, 8 * 128), np.float32)
    for e in range(8):
        oh[e, e * 128:(e + 1) * 128] = 0.5
    oh05_c = nc.inline_tensor(_bf(oh), name="oh05_c")

    import contextlib
    ctx = contextlib.ExitStack()
    sb = lambda n, s, d: ctx.enter_context(nc.sbuf_tensor(n, s, d))
    psa = lambda n, s, d: ctx.enter_context(nc.psum_tensor(n, s, d))
    mksem = lambda n: ctx.enter_context(nc.semaphore(n))

    XN = sb("XN", [128, NCH * T], BF16)
    ACC = sb("ACC", [128, NCH * TF], F32)
    WS = sb("WS", [128, 3 * 6144], BF16)
    HB = sb("HB", [128, NIC * TF], BF16)
    MI = sb("MI", [128, 4096], F32)
    CB = sb("CB", [128, 1024], BF16)
    OH05 = sb("OH05", [8, 8 * 128], BF16)
    IDENT = sb("IDENT", [128, 128], BF16)
    ONESC = sb("ONESC", [128, 1], BF16)
    ONESR = sb("ONESR", [1, 128], BF16)
    ONESRF = sb("ONESRF", [128, 128], F32)
    GWT = sb("GWT", [128, NCH * E8], BF16)
    SGWT = sb("SGWT", [128, NCH], BF16)
    BQKV = sb("BQKV", [128, 48], F32)
    BOUT = sb("BOUT", [128, 16], F32)

    P = [psa("P%d" % i, [128, 512], F32) for i in range(7)]
    PT = psa("PT", [128, 1024], BF16)

    SEMNAMES = ("pe", "ds", "as_", "cs", "kv",
                "w0", "w1", "w2", "gm0", "gm1", "qw0", "qw1",
                "sp0", "sp1", "k0", "k1", "v0", "v1", "q0",
                "xl0", "xl1", "as0", "as1", "ks0", "ks1",
                "vs0", "vs1", "xs0", "xs1")
    sems = {s: mksem(s) for s in SEMNAMES}

    pl = Plan()

    xnv = XN[:, :].rearrange("p (c t) -> p c t", t=T)
    accv = ACC[:, :].rearrange("p (c t) -> p c t", t=TF)
    ACCbf = ACC[:, :].bitcast(BF16)
    MIbf = MI[:, :].bitcast(BF16)
    wslot = [WS[:, i * 6144:(i + 1) * 6144] for i in range(3)]
    ST = MI[:, 0:1024]
    ST2 = MI[:, 1024:2048]
    A_t = MI[:, 2048:3072]
    B_t = MI[:, 3072:4096]
    DT0 = MI[:, 2048:2560]
    DT1 = MI[:, 2560:3072]
    DT2 = MI[:, 3072:3584]
    COL = [MI[:, 3584 + k:3584 + k + 1] for k in range(8)]
    SGBF = MIbf[0:1, 6144:7168]
    sgcv = lambda ic: XN[:, ic * T + TF: ic * T + 2 * TF]
    SQ_SCR = HB[:, 0:1024]
    SG_SCR = XN[:, 12 * T + TF: 12 * T + 2 * TF]
    UC_SCR = XN[:, 13 * T + TF: 13 * T + 2 * TF]
    C8_SCR = XN[:, 14 * T + TF: 14 * T + TF + 512]
    CTF = XN[0:8, 15 * T + TF: 15 * T + 2 * TF]
    CTM = MI[:, 2048:2176]

    def dma_raw(eng, out, in_, waits=(), inc=None):
        return pl.op(eng, lambda E_, o=out, i=in_: E_.dma_start(out=o, in_=i),
                     waits=waits, inc=inc, inc_by=16)

    class Stream:

        def __init__(self, eng, names):
            self.eng, self.names = eng, names
            self.i = 0

        def dma(self, out, in_, waits=()):
            s = self.names[self.i % len(self.names)]
            self.i += 1
            w = list(waits) + [(s, pl.cnt(s))]
            c = dma_raw(self.eng, out, in_, waits=w, inc=s)
            return (s, c)

        def all_waits(self):
            return [(s, pl.cnt(s)) for s in self.names]

    S_G = Stream("gpsimd", ["gm0", "gm1"])
    S_QW = Stream("gpsimd", ["qw0", "qw1"])
    S_SP = Stream("sync", ["sp0", "sp1"])
    S_K = Stream("sync", ["k0", "k1"])
    S_V = Stream("sync", ["v0", "v1"])
    S_Q = Stream("sync", ["q0"])
    S_XL = Stream("sync", ["xl0", "xl1"])
    S_AS = Stream("scalar", ["as0", "as1"])
    S_KS = Stream("scalar", ["ks0", "ks1"])
    S_VS = Stream("scalar", ["vs0", "vs1"])
    S_XS = Stream("scalar", ["xs0", "xs1"])

    bankdep = {k: ("cs", 0) for k in range(8)}

    def bfree(k, sem, cnt):
        bankdep[k] = (sem, cnt)

    def mmg(lro, waits=(), banks=()):
        w = list(waits) + [bankdep[b] for b in banks]

        def fn(E_, lro=tuple(lro)):
            r = None
            for (l, rh, o, st, sp) in lro:
                r = E_.matmul(o, l, rh, start=st, stop=sp)
            return r
        return pl.op("tensor", fn, waits=w, inc="pe")

    def tr(out, in_, waits=(), banks=(7,)):
        w = list(waits) + [bankdep[b] for b in banks]
        p = in_.shape[0]
        return pl.op("tensor",
                     lambda E_, o=out, i=in_, p=p: E_.transpose(o, i, IDENT[0:p, 0:p]),
                     waits=w, inc="pe")

    def act(fn, waits=()):
        w = list(waits) + [("as_", pl.cnt("as_"))]
        return pl.op("scalar", fn, waits=w, inc="as_")

    def dve(fn, waits=()):
        w = list(waits) + [("ds", pl.cnt("ds"))]
        return pl.op("vector", fn, waits=w, inc="ds")

    def fmv(d):
        return d.ap().rearrange("(c p) t -> p c t", p=128)

    S_SP.dma(IDENT[:, :], ident_c.ap())
    S_SP.dma(ONESC[:, :], onescol_c.ap())
    S_SP.dma(ONESR[:, :], onesrow_c.ap())
    S_SP.dma(ONESRF[:, :], onesrowf_c.ap())
    S_SP.dma(OH05[:, :], oh05_c.ap())
    S_SP.dma(BQKV[:, :], bqkv_pc_d.ap())
    S_SP.dma(BOUT[:, :], bout_pc_d.ap())
    S_G.dma(GWT[:, :], gatewT_d.ap())
    S_G.dma(SGWT[:, :], sgw_d.ap())
    CONST_W = S_SP.all_waits()
    CONST_GW = S_G.all_waits()

    WQAHEAD = 2
    wq = {"pend": [], "issued": 0, "wof": {}, "done_pe": {}, "floor": 0}

    def wq_declare(src_ap, ncols):
        wq["pend"].append((src_ap, ncols))
        return len(wq["pend"]) - 1

    def wq_pump(upto_idx):
        while wq["issued"] <= min(upto_idx, len(wq["pend"]) - 1):
            j = wq["issued"]
            src, ncols = wq["pend"][j]
            w = [("pe", max(wq["done_pe"].get(j - 3, 0), wq["floor"]))]
            s = "w%d" % (j % 3)
            w.append((s, pl.cnt(s)))
            c = dma_raw("gpsimd", wslot[j % 3][:, 0:ncols], src, waits=w, inc=s)
            wq["wof"][j] = (s, c)
            wq["issued"] = j + 1

    def wq_use(j):
        wq_pump(j + WQAHEAD)
        assert j < wq["issued"]
        return wslot[j % 3], wq["wof"][j]

    def wq_done(j):
        wq["done_pe"][j] = pl.cnt("pe")

    def emit_ln(xd, tg0, xoff=0, dep=()):
        xv = fmv(xd)
        g0 = S_G.dma(xnv[:, :, xoff:xoff + TF], xv[:, :, tg0:tg0 + TF],
                     waits=list(dep) + [("pe", pl.cnt("pe")), ("ds", pl.cnt("ds"))])
        s0 = S_SP.dma(accv[:, :, :], xv[:, :, tg0:tg0 + TF],
                      waits=list(dep) + [("ds", pl.cnt("ds")), ("as_", pl.cnt("as_"))] +
                            S_AS.all_waits())
        pe_stats = None
        for hc in range(NCH):
            a_c = act(lambda E_, s=xnv[:, hc, xoff:xoff + TF], d=SQ_SCR:
                      E_.activation(d, s, AF.Square),
                      waits=[g0, ("pe", pl.cnt("pe"))])
            lro = []
            for b in range(2):
                lro.append((ONESC[:, :], xnv[:, hc, xoff + b * 512:xoff + (b + 1) * 512],
                            P[b][0:1, :], hc == 0, hc == NCH - 1))
            for b in range(2):
                lro.append((ONESC[:, :], SQ_SCR[:, b * 512:(b + 1) * 512],
                            P[2 + b][0:1, :], hc == 0, hc == NCH - 1))
            pe_stats = mmg(lro, waits=[g0, ("as_", a_c)] + CONST_W,
                           banks=(0, 1, 2, 3) if hc == 0 else ())
        d0 = None
        for b in range(2):
            c1 = dve(lambda E_, s=P[b][0:1, :], d=ST[0:1, b * 512:(b + 1) * 512]:
                     E_.tensor_copy(d, s), waits=[("pe", pe_stats)])
            bfree(b, "ds", c1)
            d0 = dve(lambda E_, s=P[2 + b][0:1, :], d=ST2[0:1, b * 512:(b + 1) * 512]:
                     E_.tensor_copy(d, s))
            bfree(2 + b, "ds", d0)
        inv = 1.0 / float(H)
        d1 = dve(lambda E_: E_.tensor_scalar(ST[0:1, :], ST[0:1, :], inv, None, ALU.mult),
                 waits=[("ds", d0)])
        d2 = dve(lambda E_: E_.tensor_tensor(A_t[0:1, :], ST[0:1, :], ST[0:1, :], ALU.mult),
                 waits=[("ds", d1)])
        d3 = dve(lambda E_: E_.tensor_scalar(ST2[0:1, :], ST2[0:1, :], inv, None, ALU.mult),
                 waits=[("ds", d2)])
        d4 = dve(lambda E_: E_.tensor_tensor(ST2[0:1, :], ST2[0:1, :], A_t[0:1, :],
                                             ALU.subtract),
                 waits=[("ds", d3)])
        d4b = dve(lambda E_: E_.tensor_scalar(ST2[0:1, :], ST2[0:1, :], EPS, None,
                                              ALU.add),
                  waits=[("ds", d4)])
        a1 = act(lambda E_: E_.activation(B_t[0:1, :], ST2[0:1, :], AF.Sqrt),
                 waits=[("ds", d4b)])
        d5 = dve(lambda E_: E_.reciprocal(A_t[0:1, :], B_t[0:1, :]), waits=[("as_", a1)])
        d6 = dve(lambda E_: E_.tensor_tensor(B_t[0:1, :], ST[0:1, :], A_t[0:1, :],
                                             ALU.mult),
                 waits=[("ds", d5)])
        lro = []
        for b in range(2):
            lro.append((ONESRF[0:1, :], A_t[0:1, b * 512:(b + 1) * 512], P[b][:, :],
                        True, True))
            lro.append((ONESRF[0:1, :], B_t[0:1, b * 512:(b + 1) * 512], P[2 + b][:, :],
                        True, True))
        peb = mmg(lro, waits=[("ds", d6)], banks=(0, 1, 2, 3))
        dab = None
        for b in range(2):
            c1 = dve(lambda E_, s=P[b][:, :], d=A_t[:, b * 512:(b + 1) * 512]:
                     E_.tensor_copy(d, s), waits=[("pe", peb)])
            bfree(b, "ds", c1)
            dab = dve(lambda E_, s=P[2 + b][:, :], d=B_t[:, b * 512:(b + 1) * 512]:
                      E_.tensor_copy(d, s))
            bfree(2 + b, "ds", dab)
        APS = ST[:, 0:512]
        dlast = None
        for hc in range(NCH):
            for b in range(2):
                t0 = dve(lambda E_, xc=accv[:, hc, b * 512:(b + 1) * 512],
                         a=A_t[:, b * 512:(b + 1) * 512]:
                         E_.tensor_tensor(APS, xc, a, ALU.mult),
                         waits=[s0, ("ds", dab), ("pe", pe_stats)])
                dlast = dve(lambda E_, bb=B_t[:, b * 512:(b + 1) * 512],
                            d=xnv[:, hc, xoff + b * 512:xoff + (b + 1) * 512]:
                            E_.tensor_tensor(d, APS, bb, ALU.subtract),
                            waits=[("ds", t0)])
        return dlast

    def emit_gating(xn_ds):
        gwv = GWT[:, :].rearrange("p (c e) -> p c e", c=NCH)
        pe_l = None
        for tb in range(2):
            lro = []
            for hc in range(NCH):
                lro.append((gwv[:, hc, :], xnv[:, hc, tb * 512:(tb + 1) * 512],
                            P[4 + tb][0:8, :], hc == 0, hc == NCH - 1))
            pe_l = mmg(lro, waits=[("ds", xn_ds)] + CONST_GW, banks=(4 + tb,))
        pe_s = None
        for tb, bank, bi in ((0, P[6], 6), (1, P[3], 3)):
            lro = []
            for hc in range(NCH):
                lro.append((SGWT[:, hc:hc + 1], xnv[:, hc, tb * 512:(tb + 1) * 512],
                            bank[0:1, :], hc == 0, hc == NCH - 1))
            pe_s = mmg(lro, waits=[("ds", xn_ds)] + CONST_GW, banks=(bi,))
        dl = None
        for tb, bank in ((0, P[4]), (1, P[5])):
            dl = dve(lambda E_, s=bank[0:8, :], d=CTF[0:8, tb * 512:(tb + 1) * 512]:
                     E_.tensor_copy(d, s), waits=[("pe", pe_l)])
            bfree(4 + tb, "ds", dl)
        a_s = act(lambda E_: E_.activation(ST[0:1, 0:512], P[6][0:1, :], AF.Sigmoid),
                  waits=[("pe", pe_s)])
        bfree(6, "as_", a_s)
        a_s = act(lambda E_: E_.activation(ST[0:1, 512:1024], P[3][0:1, :], AF.Sigmoid))
        bfree(3, "as_", a_s)
        ds_s = dve(lambda E_: E_.tensor_scalar(SGBF[0:1, :], ST[0:1, :], 0.5, None,
                                               ALU.mult),
                   waits=[("as_", a_s)])
        dcm = None
        for tc in range(8):
            t_pe = tr(PT[:, tc * 8:(tc + 1) * 8], CTF[0:8, tc * 128:(tc + 1) * 128],
                      waits=[("ds", dl)])
            dcm = dve(lambda E_, s=PT[:, tc * 8:(tc + 1) * 8], d=CTM[:, tc * 8:(tc + 1) * 8]:
                      E_.tensor_copy(d, s), waits=[("pe", t_pe)])
            bfree(7, "ds", dcm)
        ds_prev = dcm
        for tc in range(8):
            sl = CTM[:, tc * 8:(tc + 1) * 8]
            st = CTM[:, 64 + tc * 8:64 + (tc + 1) * 8]
            d1 = dve(lambda E_, s=sl: E_.reduce_max(COL[0], s, axis=AX.X, negate=True),
                     waits=[("ds", ds_prev)])
            d6 = dve(lambda E_, s=sl, d=st: E_.tensor_scalar(d, s, COL[0], None, ALU.add),
                     waits=[("ds", d1)])
            d7 = dve(lambda E_, s=st, d=st: E_.tensor_scalar(d, s, 0.0, None, ALU.is_ge),
                     waits=[("ds", d6)])
            d8 = dve(lambda E_, s=st, d=st: E_.tensor_scalar(d, s, BIG, None, ALU.mult),
                     waits=[("ds", d7)])
            d9 = dve(lambda E_, a=sl, b=st, d=st: E_.tensor_tensor(d, a, b, ALU.subtract),
                     waits=[("ds", d8)])
            da = dve(lambda E_, s=st: E_.reduce_max(COL[1], s, axis=AX.X),
                     waits=[("ds", d9)])
            ab = act(lambda E_: E_.activation(COL[2], COL[1], AF.Exp, bias=COL[0]),
                     waits=[("ds", da)])
            db = dve(lambda E_: E_.tensor_scalar(COL[3], COL[2], 1.0, None, ALU.add),
                     waits=[("as_", ab)])
            dc = dve(lambda E_: E_.reciprocal(COL[4], COL[3]), waits=[("ds", db)])
            ac = act(lambda E_, s=sl, d=st: E_.activation(d, s, AF.Exp, bias=COL[0]),
                     waits=[("ds", dc)])
            dd = dve(lambda E_, s=sl, d=sl: E_.tensor_scalar(d, s, COL[1], None, ALU.is_ge),
                     waits=[("as_", ac)])
            de = dve(lambda E_, a=st, b=sl, d=st: E_.tensor_tensor(d, a, b, ALU.mult),
                     waits=[("ds", dd)])
            df = dve(lambda E_, s=st, d=C8_SCR[:, tc * 8:(tc + 1) * 8]:
                     E_.tensor_scalar(d, s, COL[4], None, ALU.mult),
                     waits=[("ds", de)])
            t_pe = tr(PT[0:8, 0:128], C8_SCR[:, tc * 8:(tc + 1) * 8], waits=[("ds", df)])
            ds_prev = dve(lambda E_, s=PT[0:8, 0:128],
                          d=CTF[0:8, tc * 128:(tc + 1) * 128]:
                          E_.tensor_copy(d, s), waits=[("pe", t_pe)])
            bfree(7, "ds", ds_prev)
        return ds_prev, ds_s

    def emit_cb(e, gate_ds):
        lro = []
        for tb in range(2):
            if e < E8:
                lro.append((OH05[:, e * 128:(e + 1) * 128],
                            CTF[0:8, tb * 512:(tb + 1) * 512], P[4 + tb][:, :], True, True))
            else:
                lro.append((ONESR[:, :], SGBF[0:1, tb * 512:(tb + 1) * 512],
                            P[4 + tb][:, :], True, True))
        peb = mmg(lro, waits=[("ds", gate_ds)] + CONST_W, banks=(4, 5))
        dc = None
        for tb in range(2):
            dc = dve(lambda E_, s=P[4 + tb][:, :], d=CB[:, tb * 512:(tb + 1) * 512]:
                     E_.tensor_copy(d, s), waits=[("pe", peb)])
            bfree(4 + tb, "ds", dc)
        return dc

    GU_Q = [(0, 384), (384, 384), (768, 384), (1152, 256)]
    D_Q = [(0, 512), (512, 512), (1024, 512), (1536, 512)]

    def declare_expert_pieces(e):
        if e < E8:
            wg_ap, wu_ap, wd_ap = wg_d.ap()[e], wu_d.ap()[e], wd_d.ap()[e]
        else:
            g = e - E8
            wg_ap = swg_d.ap()[:, g * II:(g + 1) * II]
            wu_ap = swu_d.ap()[:, g * II:(g + 1) * II]
            wd_ap = swd_d.ap()[g * II:(g + 1) * II, :]
        idx = {}
        for ph, wap, qs, cdim in (("g", wg_ap, GU_Q, NCH), ("u", wu_ap, GU_Q, NCH),
                                  ("d", wd_ap, D_Q, NIC)):
            for qi, (c0, cn) in enumerate(qs):
                src = wap.rearrange("(c p) i -> p c i", p=128)[:, :, c0:c0 + cn]
                idx[(ph, qi)] = (wq_declare(src, cdim * cn), c0, cn, cdim)
        return idx

    def emit_expert(e, xn_ds, cb_ds, pidx):
        def wv(ph, qi):
            j, c0, cn, cdim = pidx[(ph, qi)]
            slot, gw = wq_use(j)
            return (slot[:, 0:cdim * cn].rearrange("p (c i) -> p c i", c=cdim),
                    gw, j, c0)

        def gu_phase(ph, out_cb):
            last = None
            for ic in range(NIC):
                qi = min(ic * 128 // 384, 3)
                w, gw, j, c0 = wv(ph, qi)
                icol = ic * 128 - c0
                i0, i1 = (0, 1) if ic % 2 == 0 else (2, 3)
                b0, b1 = P[i0], P[i1]
                lro = []
                for hc in range(NCH):
                    l = w[:, hc, icol:icol + 128]
                    lro.append((l, xnv[:, hc, 0:512], b0[:, :], hc == 0, hc == NCH - 1))
                    lro.append((l, xnv[:, hc, 512:1024], b1[:, :], hc == 0, hc == NCH - 1))
                pe_c = mmg(lro, waits=[gw, ("ds", xn_ds)], banks=(i0, i1))
                if ic == NIC - 1 or min((ic + 1) * 128 // 384, 3) != qi:
                    wq_done(j)
                last = out_cb(ic, i0, i1, b0, b1, pe_c)
            return last

        def g_out(ic, i0, i1, b0, b1, pe_c):
            a0 = act(lambda E_, s=b0[:, :], d=sgcv(ic)[:, 0:512]:
                     E_.activation(d, s, AF.Silu), waits=[("pe", pe_c)])
            bfree(i0, "as_", a0)
            a1 = act(lambda E_, s=b1[:, :], d=sgcv(ic)[:, 512:1024]:
                     E_.activation(d, s, AF.Silu))
            bfree(i1, "as_", a1)
            return a1

        g_as = gu_phase("g", g_out)

        def u_out(ic, i0, i1, b0, b1, pe_c):
            c0_ = dve(lambda E_, s=b0[:, :]: E_.tensor_copy(UC_SCR[:, 0:512], s),
                      waits=[("pe", pe_c)])
            bfree(i0, "ds", c0_)
            c1_ = dve(lambda E_, s=b1[:, :]: E_.tensor_copy(UC_SCR[:, 512:1024], s))
            bfree(i1, "ds", c1_)
            m1 = dve(lambda E_, ic=ic: E_.tensor_tensor(SG_SCR[:, :], sgcv(ic), CB[:, :],
                                                        ALU.mult),
                     waits=[("ds", c1_), ("as_", g_as), ("ds", cb_ds)])
            return dve(lambda E_, ic=ic: E_.tensor_tensor(HB[:, ic * TF:(ic + 1) * TF],
                                                          SG_SCR[:, :], UC_SCR[:, :],
                                                          ALU.mult),
                       waits=[("ds", m1)])

        u_ds = gu_phase("u", u_out)

        last_acc = None
        ddt = {0: ("cs", 0), 1: ("cs", 0)}
        for hc2 in range(NCH):
            qi = hc2 * 128 // 512
            w, gw, j, c0 = wv("d", qi)
            hcol = hc2 * 128 - c0
            i0, i1 = (4, 5) if hc2 % 2 == 0 else (6, 0)
            b0, b1 = P[i0], P[i1]
            lro = []
            for ic in range(NIC):
                l = w[:, ic, hcol:hcol + 128]
                lro.append((l, HB[:, ic * TF:ic * TF + 512], b0[:, :],
                            ic == 0, ic == NIC - 1))
                lro.append((l, HB[:, ic * TF + 512:(ic + 1) * TF], b1[:, :],
                            ic == 0, ic == NIC - 1))
            pe_c = mmg(lro, waits=[gw, ("ds", u_ds)], banks=(i0, i1))
            if hc2 == NCH - 1 or (hc2 + 1) * 128 >= c0 + 512:
                wq_done(j)
            a0 = act(lambda E_, s=b0[:, :]: E_.copy(DT0, s),
                     waits=[("pe", pe_c), ddt[0]])
            bfree(i0, "as_", a0)
            a1 = act(lambda E_, s=b1[:, :]: E_.copy(DT1, s), waits=[ddt[1]])
            bfree(i1, "as_", a1)
            v0 = dve(lambda E_, hc2=hc2: E_.tensor_tensor(accv[:, hc2, 0:512],
                                                          accv[:, hc2, 0:512], DT0,
                                                          ALU.add),
                     waits=[("as_", a0)])
            ddt[0] = ("ds", v0)
            last_acc = dve(lambda E_, hc2=hc2: E_.tensor_tensor(accv[:, hc2, 512:1024],
                                                                accv[:, hc2, 512:1024],
                                                                DT1, ALU.add),
                           waits=[("as_", a1)])
            ddt[1] = ("ds", last_acc)
        return last_acc

    def emit_ffn_stage(xd, od, dep=()):
        s1 = None
        for hf in range(2):
            tg0 = hf * TF
            xn_ds = emit_ln(xd, tg0, 0, dep)
            gate_ds, sg_ds = emit_gating(xn_ds)
            pidx = [declare_expert_pieces(e) for e in range(12)]
            last = None
            for e in range(12):
                cb_ds = emit_cb(e, gate_ds if e < E8 else sg_ds)
                last = emit_expert(e, xn_ds, cb_ds, pidx[e])
            s1 = S_AS.dma(fmv(od)[:, :, tg0:tg0 + TF], accv[:, :, :],
                          waits=[("ds", last)])
        return s1

    def emit_attention(x1_dep):
        emit_ln(x1_d, 0, 0, x1_dep)
        xn2_ds = emit_ln(x1_d, TF, TF, x1_dep)

        BV = HB[:, 2048:4096]
        g_bv = S_G.dma(BV[0:1, :], bqkv_row_d.ap()[:, 2 * H:3 * H],
                       waits=[("ds", pl.cnt("ds"))])
        lro = [(ONESR[:, :], BV[0:1, b * 512:(b + 1) * 512], P[b][:, :], True, True)
               for b in range(4)]
        pebv = mmg(lro, waits=[g_bv] + CONST_W, banks=(0, 1, 2, 3))
        dbv = None
        for b in range(4):
            dbv = dve(lambda E_, b=b, s=P[b][:, :]:
                      E_.tensor_copy(BV[:, b * 512:(b + 1) * 512], s),
                      waits=[("pe", pebv)])
            bfree(b, "ds", dbv)

        wqv = wqkvT_d.ap().rearrange("(c p) o -> p c o", p=128)
        slotA_flat = ACCbf[:, 0:16384]
        slotB_flat = WS[:, 0:16384]
        QT = HB[:, 0:2048]
        qkv_done_pe = {}
        qt_store = ("cs", 0)
        vt_store = {0: ("cs", 0), 1: ("cs", 0)}
        for pi in range(6):
            flat = slotB_flat if pi >= 4 else (slotA_flat if pi % 2 == 0 else slotB_flat)
            w = [("pe", qkv_done_pe.get(pi - 2, 0)), ("ds", xn2_ds),
                 ("as_", pl.cnt("as_")), ("ds", pl.cnt("ds"))]
            if pi == 5:
                w.append(("pe", qkv_done_pe.get(4, 0)))
            if pi < 2:
                w.append(("pe", pl.cnt("pe")))
            gq_w = S_QW.dma(flat, wqv[:, :, pi * 1024:(pi + 1) * 1024], waits=w)
            slot = flat.rearrange("p (c o) -> p c o", c=NCH)
            if pi < 4:
                for oi in range(8):
                    oc = pi * 8 + oi
                    pe_c = None
                    for tb in range(4):
                        lro = []
                        for hc in range(NCH):
                            lro.append((slot[:, hc, oi * 128:(oi + 1) * 128],
                                        xnv[:, hc, tb * 512:(tb + 1) * 512],
                                        P[tb][:, :], hc == 0, hc == NCH - 1))
                        pe_c = mmg(lro, waits=[gq_w, ("ds", xn2_ds)], banks=(tb,))
                    alast = None
                    for tb in range(4):
                        alast = act(lambda E_, tb=tb, oc=oc, s=P[tb][:, :]:
                                    E_.activation(QT[:, tb * 512:(tb + 1) * 512], s,
                                                  AF.Identity, bias=BQKV[:, oc:oc + 1]),
                                    waits=[("pe", pe_c), qt_store] + CONST_W)
                        bfree(tb, "as_", alast)
                    dst = q_dram if oc < 16 else cc_in_d
                    row0 = (oc % 16) * 128
                    qt_store = S_KS.dma(dst.ap()[row0:row0 + 128, :], QT[:, :],
                                        waits=[("as_", alast)])
                qkv_done_pe[pi] = pl.cnt("pe")
            else:
                VT = HB[:, 4096 + (pi - 4) * 1024: 4096 + (pi - 3) * 1024]
                for tc in range(NCH):
                    pe_c = None
                    for ob in range(2):
                        lro = []
                        for hc in range(NCH):
                            lro.append((xnv[:, hc, tc * 128:(tc + 1) * 128],
                                        slot[:, hc, ob * 512:(ob + 1) * 512],
                                        P[4 + ob][:, :], hc == 0, hc == NCH - 1))
                        pe_c = mmg(lro, waits=[gq_w, ("ds", xn2_ds)], banks=(4 + ob,))
                    dl = None
                    for ob in range(2):
                        c0_ = dve(lambda E_, ob=ob, s=P[4 + ob][:, :], vt=VT:
                                  E_.tensor_copy(vt[:, ob * 512:(ob + 1) * 512], s),
                                  waits=[("pe", pe_c), vt_store[pi - 4]])
                        bfree(4 + ob, "ds", c0_)
                        dl = dve(lambda E_, ob=ob, pi=pi, vt=VT:
                                 E_.tensor_tensor(vt[:, ob * 512:(ob + 1) * 512],
                                                  vt[:, ob * 512:(ob + 1) * 512],
                                                  BV[:, (pi - 4) * 1024 + ob * 512:
                                                     (pi - 4) * 1024 + (ob + 1) * 512],
                                                  ALU.add),
                                 waits=[("ds", c0_), ("ds", dbv)])
                    vt_store[pi - 4] = S_VS.dma(
                        cc_in_d.ap()[T + tc * 128:T + (tc + 1) * 128,
                                     (pi - 4) * 1024:(pi - 3) * 1024],
                        VT[:, :], waits=[("ds", dl)])
                qkv_done_pe[pi] = pl.cnt("pe")

        kv_waits = S_KS.all_waits() + S_VS.all_waits()
        cs1 = pl.op("gpsimd",
                    lambda E_: E_.collective_compute(
                        "AllGather", ALU.bypass,
                        replica_groups=[list(range(world))],
                        ins=[cc_in_d.ap()], outs=[cc_out_d.ap()]),
                    waits=kv_waits, inc="cs", inc_by=1)

        def sw_fn(E_):
            pid = nc.partition_id([mybir.EngineType.Pool])
            for i in nc.Switch(engines=[E_], index=[pid], n=8):
                p = (i ^ 1) % world
                E_.dma_start(out=kvp_d.ap(),
                             in_=cc_out_d.ap()[p * 2 * T:(p + 1) * 2 * T, :]
                             ).then_inc(sems["kv"], 16)
        pl.op("gpsimd", sw_fn, waits=[("cs", pl.cnt("cs"))])
        kvp_w = ("kv", pl.bump("kv", 16))

        QTILE = ACCbf[:, 0:8192].rearrange("p (c t) -> p c t", c=NCH)
        SCq = [ACCbf[:, 8192 + q * 4096:8192 + (q + 1) * 4096] for q in range(4)]
        X1C = [ACC[:, 12288 + i * 512:12288 + (i + 1) * 512] for i in range(2)]
        PR = XN[:, 0:4096]
        PRT = XN[:, 4096:20480].rearrange("p (k t) -> p k t", k=32)
        CTX = XN[:, 20480:28672].rearrange("p (c t) -> p c t", c=NCH)
        KT = [WS[:, 0:8192], WS[:, 8192:16384]]
        VT4 = [HB[:, 4096 + i * 3584:4096 + (i + 1) * 3584] for i in range(2)]

        qv = q_dram.ap().rearrange("(c p) t -> p c t", p=128)
        kv_own = cc_in_d.ap()[0:T, :].rearrange("(c p) t -> p c t", p=128)
        kv_par = kvp_d.ap()[0:T, :].rearrange("(c p) t -> p c t", p=128)
        x1v = fmv(x1_d)
        x2v = fmv(x2_d)
        x2_store = {}
        last_x2 = None
        pe_qkv = pl.cnt("pe")
        kt_done = {0: pe_qkv, 1: pe_qkv}
        vt4_done = {0: pe_qkv, 1: pe_qkv}
        scq_free = {qt: ("pe", pe_qkv) for qt in range(4)}
        qtile_done = pe_qkv
        pr_done = 0
        dt_free = {0: ("cs", 0), 1: ("cs", 0)}

        for qb in range(4):
            gq_q = S_Q.dma(ACCbf[:, 0:8192], qv[:, :, qb * 512:(qb + 1) * 512],
                           waits=[("pe", qtile_done)] +
                                 (S_KS.all_waits() if qb == 0 else []))
            for kb in range(8):
                khalf = kb // 4
                src2 = kv_own if khalf == 0 else kv_par
                kw = [("pe", kt_done[kb % 2])]
                if khalf == 1:
                    kw.append(kvp_w)
                else:
                    kw += S_KS.all_waits() + S_VS.all_waits()
                gk = S_K.dma(KT[kb % 2],
                             src2[:, :, (kb % 4) * 512:((kb % 4) + 1) * 512], waits=kw)
                ktv = KT[kb % 2].rearrange("p (c t) -> p c t", c=NCH)
                for qt in range(4):
                    bi = qt if kb % 2 == 0 else (qt + 4) % 7
                    lro = []
                    for hc in range(NCH):
                        lro.append((QTILE[:, hc, qt * 128:(qt + 1) * 128],
                                    ktv[:, hc, :], P[bi][:, :], hc == 0, hc == NCH - 1))
                    pe_c = mmg(lro, waits=[gk, gq_q], banks=(bi,))
                    cw = [("pe", pe_c)]
                    if kb == 0:
                        cw.append(scq_free[qt])
                    cc2 = dve(lambda E_, s=P[bi][:, :], qt=qt, kb=kb:
                              E_.tensor_copy(SCq[qt][:, kb * 512:(kb + 1) * 512], s),
                              waits=cw)
                    bfree(bi, "ds", cc2)
                kt_done[kb % 2] = pl.cnt("pe")
            qtile_done = pl.cnt("pe")
            for qt in range(4):
                d1 = dve(lambda E_, qt=qt: E_.reduce_max(COL[5], SCq[qt], axis=AX.X,
                                                         negate=True))
                d2 = dve(lambda E_: E_.tensor_scalar(COL[6], COL[5], ISCL, None, ALU.mult),
                         waits=[("ds", d1)])
                a1 = act(lambda E_, qt=qt: E_.activation(PR[:, :], SCq[qt], AF.Exp,
                                                         bias=COL[6], scale=ISCL,
                                                         accum_out=COL[7]),
                         waits=[("ds", d2), ("pe", pr_done)])
                scq_free[qt] = ("as_", a1)
                d3 = dve(lambda E_: E_.reciprocal(COL[5], COL[7]), waits=[("as_", a1)])
                d4 = dve(lambda E_: E_.tensor_scalar(PR[:, :], PR[:, :], COL[5], None,
                                                     ALU.mult),
                         waits=[("ds", d3)])
                for kg in range(4):
                    pe_t = None
                    for kk in range(8):
                        ktc = kg * 8 + kk
                        pe_t = tr(PT[:, kk * 128:(kk + 1) * 128],
                                  PR[:, ktc * 128:(ktc + 1) * 128],
                                  waits=[("ds", d4)], banks=(7,) if kk == 0 else ())
                    prc = dve(lambda E_, kg=kg, qt=qt,
                              s=PT[:, :].rearrange("p (k t) -> p k t", k=8):
                              E_.tensor_copy(PRT[:, kg * 8:(kg + 1) * 8,
                                                 qt * 128:(qt + 1) * 128], s),
                              waits=[("pe", pe_t)])
                    bfree(7, "ds", prc)
                pr_done = pl.cnt("pe")
            prt_ds = pl.cnt("ds")
            ctx_as = None
            for gg, (h0, hn) in enumerate(((0, 7), (7, 7), (14, 2))):
                pe_c = None
                for kt4 in range(8):
                    khalf = kt4 // 4
                    src_d = cc_in_d if khalf == 0 else kvp_d
                    r0 = T + (kt4 % 4) * 512
                    vw = [("pe", vt4_done[kt4 % 2])]
                    if khalf == 1:
                        vw.append(kvp_w)
                    else:
                        vw += S_VS.all_waits()
                    gv = S_V.dma(VT4[kt4 % 2][:, 0:4 * hn * 128],
                                 src_d.ap()[r0:r0 + 512, h0 * 128:(h0 + hn) * 128]
                                 .rearrange("(c p) h -> p c h", p=128),
                                 waits=vw)
                    vt = VT4[kt4 % 2][:, 0:4 * hn * 128].rearrange(
                        "p (c h) -> p c h", c=4)
                    for c4 in range(4):
                        ktc = kt4 * 4 + c4
                        lro = []
                        for hh in range(hn):
                            lro.append((vt[:, c4, hh * 128:(hh + 1) * 128],
                                        PRT[:, ktc, :], P[hh][:, :],
                                        ktc == 0, ktc == 31))
                        pe_c = mmg(lro, waits=[gv, ("ds", prt_ds)],
                                   banks=tuple(range(hn)) if ktc == 0 else ())
                    vt4_done[kt4 % 2] = pl.cnt("pe")
                for hh in range(hn):
                    ca = act(lambda E_, hh=hh, h0=h0, s=P[hh][:, :]:
                             E_.copy(CTX[:, h0 + hh, :], s),
                             waits=[("pe", pe_c)])
                    bfree(hh, "as_", ca)
                    ctx_as = ca
            for pi in range(4):
                gw_w = S_QW.dma(KT[pi % 2][:, 0:8192],
                                woutT_d.ap().rearrange("(c p) o -> p c o", p=128)
                                [:, :, pi * 512:(pi + 1) * 512],
                                waits=[("pe", kt_done[pi % 2])])
                wot = KT[pi % 2].rearrange("p (c o) -> p c o", c=NCH)
                for oi in range(4):
                    oc = pi * 4 + oi
                    bi = oc % 2
                    lro = []
                    for hc in range(NCH):
                        lro.append((wot[:, hc, oi * 128:(oi + 1) * 128],
                                    CTX[:, hc, :], P[bi][:, :], hc == 0, hc == NCH - 1))
                    pe_c = mmg(lro, waits=[gw_w, ("as_", ctx_as)], banks=(bi,))
                    dtt = DT0 if bi == 0 else DT1
                    ao = act(lambda E_, s=P[bi][:, :], oc=oc, d=dtt:
                             E_.activation(d, s, AF.Identity, bias=BOUT[:, oc:oc + 1]),
                             waits=[("pe", pe_c), dt_free[bi]] + CONST_W)
                    bfree(bi, "as_", ao)
                    gx = S_XL.dma(X1C[oc % 2], x1v[:, oc, qb * 512:(qb + 1) * 512],
                                  waits=[x2_store.get(oc % 2, ("cs", 0))])
                    dx = dve(lambda E_, oc=oc, d=dtt: E_.tensor_tensor(
                        X1C[oc % 2], X1C[oc % 2], d, ALU.add),
                             waits=[("as_", ao), gx])
                    dt_free[bi] = ("ds", dx)
                    sx = S_XS.dma(x2v[:, oc, qb * 512:(qb + 1) * 512],
                                  X1C[oc % 2], waits=[("ds", dx)])
                    x2_store[oc % 2] = sx
                    last_x2 = sx
                kt_done[pi % 2] = pl.cnt("pe")
        return last_x2

    if upto == "ffn1":
        emit_ffn_stage(x0_d, y_d)
    elif upto == "attn":
        emit_ffn_stage(x0_d, x1_d)
        x1w = S_AS.all_waits()
        emit_attention(x1w)
        S_AS.dma(y_d.ap(), x2_d.ap(), waits=S_XS.all_waits())
    else:
        emit_ffn_stage(x0_d, x1_d)
        x1w = S_AS.all_waits()
        emit_attention(x1w)
        wq["floor"] = pl.cnt("pe")
        emit_ffn_stage(x2_d, y_d, dep=S_XS.all_waits())

    if dbg:
        if upto != "ffn1":
            dbg_cco = nc.dram_tensor("dbg_cco", [world * 2 * T, H], BF16,
                                     kind="ExternalOutput")
            dbg_kvp = nc.dram_tensor("dbg_kvp", [2 * T, H], BF16, kind="ExternalOutput")
            dbg_q = nc.dram_tensor("dbg_q", [H, T], BF16, kind="ExternalOutput")
            dbg_cci = nc.dram_tensor("dbg_cci", [2 * T, H], BF16, kind="ExternalOutput")
            dwx = [("pe", pl.cnt("pe")), ("ds", pl.cnt("ds")), ("as_", pl.cnt("as_"))]
            S_AS.dma(dbg_cco.ap(), cc_out_d.ap(), waits=dwx)
            S_AS.dma(dbg_kvp.ap(), kvp_d.ap(), waits=dwx)
            S_AS.dma(dbg_q.ap(), q_dram.ap(), waits=dwx)
            S_AS.dma(dbg_cci.ap(), cc_in_d.ap(), waits=dwx)
        dbg_xn = nc.dram_tensor("dbg_xn", [128, NCH * TF], BF16, kind="ExternalOutput")
        dbg_cb = nc.dram_tensor("dbg_cb", [128, 1024], BF16, kind="ExternalOutput")
        dbg_hb = nc.dram_tensor("dbg_hb", [128, NIC * TF], BF16, kind="ExternalOutput")

        dw = [("pe", pl.cnt("pe")), ("ds", pl.cnt("ds")), ("as_", pl.cnt("as_"))]
        S_AS.dma(dbg_xn.ap().rearrange("p (c t) -> p c t", t=TF), xnv[:, :, 0:TF],
                 waits=dw)
        S_AS.dma(dbg_cb.ap(), CB[:, :], waits=dw)
        S_AS.dma(dbg_hb.ap(), HB[:, :], waits=dw)


    fin = S_AS.all_waits() + S_XS.all_waits()
    pl.op("scalar", lambda E_: E_.nop(), waits=fin)

    with nc.Block() as block:
        def mk(name):
            def _(E_):
                for fn, wl, inc, inc_by in pl.ops[name]:
                    for s, v in wl:
                        E_.wait_ge(sems[s], v)
                    r = fn(E_)
                    if inc is not None and r is not None:
                        r.then_inc(sems[inc], inc_by)
            return _
        block.sync(mk("sync"))
        block.gpsimd(mk("gpsimd"))
        block.tensor(mk("tensor"))
        block.vector(mk("vector"))
        block.scalar(mk("scalar"))
    ctx.close()
    return nc


_NC_CACHE = {}


def kernel(L0_attn_hidden_states, L0_attn_w_qkv, L0_attn_b_qkv, L0_attn_w_out,
           L0_attn_b_out, L0_ffn_gate_w, L0_ffn_w_gate, L0_ffn_w_up, L0_ffn_w_down,
           L0_ffn_sw_gate, L0_ffn_sw_up, L0_ffn_sw_down, L0_ffn_shared_gate_w):
    x0 = np.asarray(L0_attn_hidden_states, dtype=np.float32)
    w_qkv = np.asarray(L0_attn_w_qkv, np.float32)
    b_qkv = np.asarray(L0_attn_b_qkv, np.float32)
    w_out = np.asarray(L0_attn_w_out, np.float32)
    b_out = np.asarray(L0_attn_b_out, np.float32)
    gate_w = np.asarray(L0_ffn_gate_w, np.float32)

    shared = {
        "wqkvT": np.ascontiguousarray(w_qkv.T),
        "bqkv_pc": np.ascontiguousarray(b_qkv.reshape(48, 128).T),
        "bqkv_row": np.ascontiguousarray(b_qkv.reshape(1, 3 * H)),
        "woutT": np.ascontiguousarray(w_out.T),
        "bout_pc": np.ascontiguousarray(b_out.reshape(16, 128).T),
        "gatew_pc": np.ascontiguousarray(
            gate_w.T.reshape(16, 128, 8).transpose(1, 0, 2).reshape(128, 128)),
        "sgw_pc": np.ascontiguousarray(
            np.asarray(L0_ffn_shared_gate_w, np.float32).reshape(16, 128).T),
        "wg": np.ascontiguousarray(np.asarray(L0_ffn_w_gate, np.float32)),
        "wu": np.ascontiguousarray(np.asarray(L0_ffn_w_up, np.float32)),
        "wd": np.ascontiguousarray(np.asarray(L0_ffn_w_down, np.float32)),
        "swg": np.ascontiguousarray(np.asarray(L0_ffn_sw_gate, np.float32)),
        "swu": np.ascontiguousarray(np.asarray(L0_ffn_sw_up, np.float32)),
        "swd": np.ascontiguousarray(np.asarray(L0_ffn_sw_down, np.float32)),
    }
    in_maps = []
    for c in range(8):
        b, hf = c // 2, c % 2
        m = dict(shared)
        m["x0"] = np.ascontiguousarray(x0[b, hf * T:(hf + 1) * T, :].T)
        in_maps.append(m)

    if "nc" not in _NC_CACHE:
        _NC_CACHE["nc"] = build_program("full")
    res = run_bass_kernel_spmd(_NC_CACHE["nc"], in_maps, core_ids=list(range(8)))
    out = np.empty((4, 2 * T, H), np.float32)
    for c in range(8):
        b, hf = c // 2, c % 2
        out[b, hf * T:(hf + 1) * T, :] = np.asarray(res.results[c]["y"]).T
    return out


# version 03
# speedup vs baseline: 1.4775x; 1.0098x over previous
import math
import numpy as np
import concourse.bass as bass
import concourse.mybir as mybir
from concourse.bass_utils import run_bass_kernel_spmd

F32 = mybir.dt.float32
BF16 = mybir.dt.bfloat16
AF = mybir.ActivationFunctionType
ALU = mybir.AluOpType
AX = mybir.AxisListType

H = 2048
NCH = 16
T = 2048
TF = 1024
E8 = 8
II = 1408
NIC = 11
EPS = 1e-5
ISCL = 1.0 / math.sqrt(2048.0)
BIG = 1.0e30

ENGS = ("sync", "gpsimd", "tensor", "vector", "scalar")


class Plan:
    def __init__(self):
        self.ops = {e: [] for e in ENGS}
        self.sems = {}
        self.floors = {e: {} for e in ENGS}

    def op(self, eng, fn, waits=(), inc=None, inc_by=1):
        wl = []
        for s, v in waits:
            if v is None or v <= 0:
                continue
            if self.floors[eng].get(s, 0) < v:
                self.floors[eng][s] = v
                wl.append((s, v))
        if inc is not None:
            self.sems[inc] = self.sems.get(inc, 0) + inc_by
        self.ops[eng].append((fn, wl, inc, inc_by))
        return self.sems.get(inc) if inc is not None else None

    def bump(self, s, n):
        self.sems[s] = self.sems.get(s, 0) + n
        return self.sems[s]

    def cnt(self, s):
        return self.sems.get(s, 0)


def _bf(a):
    import ml_dtypes
    return np.ascontiguousarray(a).astype(ml_dtypes.bfloat16)


def build_program(upto="full", dbg=False, world=8):
    nc = bass.Bass("TRN2", target_bir_lowering=False)

    x0_d = nc.dram_tensor("x0", [H, T], F32, kind="ExternalInput")
    wqkvT_d = nc.dram_tensor("wqkvT", [H, 3 * H], F32, kind="ExternalInput")
    bqkv_pc_d = nc.dram_tensor("bqkv_pc", [128, 48], F32, kind="ExternalInput")
    bqkv_row_d = nc.dram_tensor("bqkv_row", [1, 3 * H], F32, kind="ExternalInput")
    woutT_d = nc.dram_tensor("woutT", [H, H], F32, kind="ExternalInput")
    bout_pc_d = nc.dram_tensor("bout_pc", [128, 16], F32, kind="ExternalInput")
    gatewT_d = nc.dram_tensor("gatew_pc", [128, NCH * E8], F32, kind="ExternalInput")
    wg_d = nc.dram_tensor("wg", [E8, H, II], F32, kind="ExternalInput")
    wu_d = nc.dram_tensor("wu", [E8, H, II], F32, kind="ExternalInput")
    wd_d = nc.dram_tensor("wd", [E8, II, H], F32, kind="ExternalInput")
    swg_d = nc.dram_tensor("swg", [H, 4 * II], F32, kind="ExternalInput")
    swu_d = nc.dram_tensor("swu", [H, 4 * II], F32, kind="ExternalInput")
    swd_d = nc.dram_tensor("swd", [4 * II, H], F32, kind="ExternalInput")
    sgw_d = nc.dram_tensor("sgw_pc", [128, NCH], F32, kind="ExternalInput")

    y_d = nc.dram_tensor("y", [H, T], F32, kind="ExternalOutput")

    x1_d = nc.dram_tensor("x1buf", [H, T], F32, kind="Internal")
    x2_d = nc.dram_tensor("x2buf", [H, T], F32, kind="Internal")
    q_dram = nc.dram_tensor("qbuf", [H, T], BF16, kind="Internal")
    cc_in_d = nc.dram_tensor("cc_in", [2 * T, H], BF16, kind="Internal")
    cc_out_d = nc.dram_tensor("cc_out", [world * 2 * T, H], BF16, kind="Internal",
                              addr_space="Shared" if world == 8 else "Local")
    kvp_d = nc.dram_tensor("kvp", [2 * T, H], BF16, kind="Internal")

    ident_c = nc.inline_tensor(_bf(np.eye(128, dtype=np.float32)), name="ident_c")
    onescol_c = nc.inline_tensor(_bf(np.ones((128, 1), np.float32)), name="onescol_c")
    onesrow_c = nc.inline_tensor(_bf(np.ones((1, 128), np.float32)), name="onesrow_c")
    onesrowf_c = nc.inline_tensor(np.ones((128, 128), np.float32), name="onesrowf_c")
    oh = np.zeros((8, 8 * 128), np.float32)
    for e in range(8):
        oh[e, e * 128:(e + 1) * 128] = 0.5
    oh05_c = nc.inline_tensor(_bf(oh), name="oh05_c")

    import contextlib
    ctx = contextlib.ExitStack()
    sb = lambda n, s, d: ctx.enter_context(nc.sbuf_tensor(n, s, d))
    psa = lambda n, s, d: ctx.enter_context(nc.psum_tensor(n, s, d))
    mksem = lambda n: ctx.enter_context(nc.semaphore(n))

    XN = sb("XN", [128, NCH * T], BF16)
    ACC = sb("ACC", [128, NCH * TF], F32)
    WS = sb("WS", [128, 3 * 6144], BF16)
    HB = sb("HB", [128, NIC * TF], BF16)
    MI = sb("MI", [128, 4096], F32)
    CB = sb("CB", [128, 1024], BF16)
    OH05 = sb("OH05", [8, 8 * 128], BF16)
    IDENT = sb("IDENT", [128, 128], BF16)
    ONESC = sb("ONESC", [128, 1], BF16)
    ONESR = sb("ONESR", [1, 128], BF16)
    ONESRF = sb("ONESRF", [128, 128], F32)
    GWT = sb("GWT", [128, NCH * E8], BF16)
    SGWT = sb("SGWT", [128, NCH], BF16)
    BQKV = sb("BQKV", [128, 48], F32)
    BOUT = sb("BOUT", [128, 16], F32)

    P = [psa("P%d" % i, [128, 512], F32) for i in range(7)]
    PT = psa("PT", [128, 1024], BF16)

    SEMNAMES = ("pe", "ds", "as_", "cs", "kv",
                "w0", "w1", "w2", "gm0", "gm1", "qw0", "qw1",
                "sp0", "sp1", "k0", "k1", "v0", "v1", "q0",
                "xl0", "xl1", "as0", "as1", "ks0", "ks1",
                "vs0", "vs1", "xs0", "xs1")
    sems = {s: mksem(s) for s in SEMNAMES}

    pl = Plan()

    xnv = XN[:, :].rearrange("p (c t) -> p c t", t=T)
    accv = ACC[:, :].rearrange("p (c t) -> p c t", t=TF)
    ACCbf = ACC[:, :].bitcast(BF16)
    MIbf = MI[:, :].bitcast(BF16)
    wslot = [WS[:, i * 6144:(i + 1) * 6144] for i in range(3)]
    ST = MI[:, 0:1024]
    ST2 = MI[:, 1024:2048]
    A_t = MI[:, 2048:3072]
    B_t = MI[:, 3072:4096]
    DT0 = MI[:, 2048:2560]
    DT1 = MI[:, 2560:3072]
    DT2 = MI[:, 3072:3584]
    COL = [MI[:, 3584 + k:3584 + k + 1] for k in range(8)]
    SGBF = MIbf[0:1, 6144:7168]
    sgcv = lambda ic: XN[:, ic * T + TF: ic * T + 2 * TF]
    SQ_SCR2 = [HB[:, 0:1024], HB[:, 1024:2048]]
    SG_SCR = XN[:, 12 * T + TF: 12 * T + 2 * TF]
    UC_SCR = XN[:, 13 * T + TF: 13 * T + 2 * TF]
    C8_SCR = XN[:, 14 * T + TF: 14 * T + TF + 512]
    CTF = XN[0:8, 15 * T + TF: 15 * T + 2 * TF]
    CTM = MI[:, 2048:2176]

    def dma_raw(eng, out, in_, waits=(), inc=None):
        return pl.op(eng, lambda E_, o=out, i=in_: E_.dma_start(out=o, in_=i),
                     waits=waits, inc=inc, inc_by=16)

    class Stream:

        def __init__(self, eng, names):
            self.eng, self.names = eng, names
            self.i = 0

        def dma(self, out, in_, waits=()):
            s = self.names[self.i % len(self.names)]
            self.i += 1
            w = list(waits) + [(s, pl.cnt(s))]
            c = dma_raw(self.eng, out, in_, waits=w, inc=s)
            return (s, c)

        def all_waits(self):
            return [(s, pl.cnt(s)) for s in self.names]

    S_G = Stream("gpsimd", ["gm0", "gm1"])
    S_QW = Stream("gpsimd", ["qw0", "qw1"])
    S_SP = Stream("sync", ["sp0", "sp1"])
    S_K = Stream("sync", ["k0", "k1"])
    S_V = Stream("sync", ["v0", "v1"])
    S_Q = Stream("sync", ["q0"])
    S_XL = Stream("sync", ["xl0", "xl1"])
    S_AS = Stream("scalar", ["as0", "as1"])
    S_KS = Stream("scalar", ["ks0", "ks1"])
    S_VS = Stream("scalar", ["vs0", "vs1"])
    S_XS = Stream("scalar", ["xs0", "xs1"])

    bankdep = {k: ("cs", 0) for k in range(8)}

    def bfree(k, sem, cnt):
        bankdep[k] = (sem, cnt)

    def mmg(lro, waits=(), banks=()):
        w = list(waits) + [bankdep[b] for b in banks]

        def fn(E_, lro=tuple(lro)):
            r = None
            for (l, rh, o, st, sp) in lro:
                r = E_.matmul(o, l, rh, start=st, stop=sp)
            return r
        return pl.op("tensor", fn, waits=w, inc="pe")

    def tr(out, in_, waits=(), banks=(7,)):
        w = list(waits) + [bankdep[b] for b in banks]
        p = in_.shape[0]
        return pl.op("tensor",
                     lambda E_, o=out, i=in_, p=p: E_.transpose(o, i, IDENT[0:p, 0:p]),
                     waits=w, inc="pe")

    def act(fn, waits=()):
        w = list(waits) + [("as_", pl.cnt("as_"))]
        return pl.op("scalar", fn, waits=w, inc="as_")

    def dve(fn, waits=()):
        w = list(waits) + [("ds", pl.cnt("ds"))]
        return pl.op("vector", fn, waits=w, inc="ds")

    def fmv(d):
        return d.ap().rearrange("(c p) t -> p c t", p=128)

    S_SP.dma(IDENT[:, :], ident_c.ap())
    S_SP.dma(ONESC[:, :], onescol_c.ap())
    S_SP.dma(ONESR[:, :], onesrow_c.ap())
    S_SP.dma(ONESRF[:, :], onesrowf_c.ap())
    S_SP.dma(OH05[:, :], oh05_c.ap())
    S_SP.dma(BQKV[:, :], bqkv_pc_d.ap())
    S_SP.dma(BOUT[:, :], bout_pc_d.ap())
    S_G.dma(GWT[:, :], gatewT_d.ap())
    S_G.dma(SGWT[:, :], sgw_d.ap())
    CONST_W = S_SP.all_waits()
    CONST_GW = S_G.all_waits()

    WQAHEAD = 2
    wq = {"pend": [], "issued": 0, "wof": {}, "done_pe": {}, "floor": 0}

    def wq_declare(src_ap, ncols):
        wq["pend"].append((src_ap, ncols))
        return len(wq["pend"]) - 1

    def wq_pump(upto_idx):
        while wq["issued"] <= min(upto_idx, len(wq["pend"]) - 1):
            j = wq["issued"]
            src, ncols = wq["pend"][j]
            w = [("pe", max(wq["done_pe"].get(j - 3, 0), wq["floor"]))]
            s = "w%d" % (j % 3)
            w.append((s, pl.cnt(s)))
            c = dma_raw("gpsimd", wslot[j % 3][:, 0:ncols], src, waits=w, inc=s)
            wq["wof"][j] = (s, c)
            wq["issued"] = j + 1

    def wq_use(j):
        wq_pump(j + WQAHEAD)
        assert j < wq["issued"]
        return wslot[j % 3], wq["wof"][j]

    def wq_done(j):
        wq["done_pe"][j] = pl.cnt("pe")

    def emit_ln(xd, tg0, xoff=0, dep=(), xn_free_pe=None, acc_free=()):
        xv = fmv(xd)
        if xn_free_pe is None:
            xn_free_pe = pl.cnt("pe")
        g0 = S_G.dma(xnv[:, :, xoff:xoff + TF], xv[:, :, tg0:tg0 + TF],
                     waits=list(dep) + [("pe", xn_free_pe), ("ds", pl.cnt("ds"))])
        s0 = S_SP.dma(accv[:, :, :], xv[:, :, tg0:tg0 + TF],
                      waits=list(dep) + list(acc_free) + [("ds", pl.cnt("ds"))] +
                            S_AS.all_waits())
        pe_stats = None
        sq_as = {0: 0, 1: 0}
        for hc in range(NCH):
            scr = SQ_SCR2[hc % 2]
            a_c = act(lambda E_, s=xnv[:, hc, xoff:xoff + TF], d=scr:
                      E_.activation(d, s, AF.Square),
                      waits=[g0, ("pe", pe_stats or xn_free_pe)])
            sq_as[hc % 2] = a_c
            lro = []
            for b in range(2):
                lro.append((ONESC[:, :], xnv[:, hc, xoff + b * 512:xoff + (b + 1) * 512],
                            P[b][0:1, :], hc == 0, hc == NCH - 1))
            for b in range(2):
                lro.append((ONESC[:, :], scr[:, b * 512:(b + 1) * 512],
                            P[2 + b][0:1, :], hc == 0, hc == NCH - 1))
            pe_stats = mmg(lro, waits=[g0, ("as_", a_c)] + CONST_W,
                           banks=(0, 1, 2, 3) if hc == 0 else ())
        d0 = None
        for b in range(2):
            c1 = dve(lambda E_, s=P[b][0:1, :], d=ST[0:1, b * 512:(b + 1) * 512]:
                     E_.tensor_copy(d, s), waits=[("pe", pe_stats)])
            bfree(b, "ds", c1)
            d0 = dve(lambda E_, s=P[2 + b][0:1, :], d=ST2[0:1, b * 512:(b + 1) * 512]:
                     E_.tensor_copy(d, s))
            bfree(2 + b, "ds", d0)
        inv = 1.0 / float(H)
        d1 = dve(lambda E_: E_.tensor_scalar(ST[0:1, :], ST[0:1, :], inv, None, ALU.mult),
                 waits=[("ds", d0)])
        d2 = dve(lambda E_: E_.tensor_tensor(A_t[0:1, :], ST[0:1, :], ST[0:1, :], ALU.mult),
                 waits=[("ds", d1)])
        d3 = dve(lambda E_: E_.tensor_scalar(ST2[0:1, :], ST2[0:1, :], inv, None, ALU.mult),
                 waits=[("ds", d2)])
        d4 = dve(lambda E_: E_.tensor_tensor(ST2[0:1, :], ST2[0:1, :], A_t[0:1, :],
                                             ALU.subtract),
                 waits=[("ds", d3)])
        d4b = dve(lambda E_: E_.tensor_scalar(ST2[0:1, :], ST2[0:1, :], EPS, None,
                                              ALU.add),
                  waits=[("ds", d4)])
        a1 = act(lambda E_: E_.activation(B_t[0:1, :], ST2[0:1, :], AF.Sqrt),
                 waits=[("ds", d4b)])
        d5 = dve(lambda E_: E_.reciprocal(A_t[0:1, :], B_t[0:1, :]), waits=[("as_", a1)])
        d6 = dve(lambda E_: E_.tensor_tensor(B_t[0:1, :], ST[0:1, :], A_t[0:1, :],
                                             ALU.mult),
                 waits=[("ds", d5)])
        lro = []
        for b in range(2):
            lro.append((ONESRF[0:1, :], A_t[0:1, b * 512:(b + 1) * 512], P[b][:, :],
                        True, True))
            lro.append((ONESRF[0:1, :], B_t[0:1, b * 512:(b + 1) * 512], P[2 + b][:, :],
                        True, True))
        peb = mmg(lro, waits=[("ds", d6)], banks=(0, 1, 2, 3))
        dab = None
        for b in range(2):
            c1 = dve(lambda E_, s=P[b][:, :], d=A_t[:, b * 512:(b + 1) * 512]:
                     E_.tensor_copy(d, s), waits=[("pe", peb)])
            bfree(b, "ds", c1)
            dab = dve(lambda E_, s=P[2 + b][:, :], d=B_t[:, b * 512:(b + 1) * 512]:
                      E_.tensor_copy(d, s))
            bfree(2 + b, "ds", dab)
        APS = ST[:, 0:512]
        dlast = None
        for hc in range(NCH):
            for b in range(2):
                t0 = dve(lambda E_, xc=accv[:, hc, b * 512:(b + 1) * 512],
                         a=A_t[:, b * 512:(b + 1) * 512]:
                         E_.tensor_tensor(APS, xc, a, ALU.mult),
                         waits=[s0, ("ds", dab), ("pe", pe_stats)])
                dlast = dve(lambda E_, bb=B_t[:, b * 512:(b + 1) * 512],
                            d=xnv[:, hc, xoff + b * 512:xoff + (b + 1) * 512]:
                            E_.tensor_tensor(d, APS, bb, ALU.subtract),
                            waits=[("ds", t0)])
        return dlast

    def emit_gating(xn_ds):
        gwv = GWT[:, :].rearrange("p (c e) -> p c e", c=NCH)
        pe_l = None
        for tb in range(2):
            lro = []
            for hc in range(NCH):
                lro.append((gwv[:, hc, :], xnv[:, hc, tb * 512:(tb + 1) * 512],
                            P[4 + tb][0:8, :], hc == 0, hc == NCH - 1))
            pe_l = mmg(lro, waits=[("ds", xn_ds)] + CONST_GW, banks=(4 + tb,))
        pe_s = None
        for tb, bank, bi in ((0, P[6], 6), (1, P[3], 3)):
            lro = []
            for hc in range(NCH):
                lro.append((SGWT[:, hc:hc + 1], xnv[:, hc, tb * 512:(tb + 1) * 512],
                            bank[0:1, :], hc == 0, hc == NCH - 1))
            pe_s = mmg(lro, waits=[("ds", xn_ds)] + CONST_GW, banks=(bi,))
        dl = None
        for tb, bank in ((0, P[4]), (1, P[5])):
            dl = dve(lambda E_, s=bank[0:8, :], d=CTF[0:8, tb * 512:(tb + 1) * 512]:
                     E_.tensor_copy(d, s), waits=[("pe", pe_l)])
            bfree(4 + tb, "ds", dl)
        a_s = act(lambda E_: E_.activation(ST[0:1, 0:512], P[6][0:1, :], AF.Sigmoid),
                  waits=[("pe", pe_s)])
        bfree(6, "as_", a_s)
        a_s = act(lambda E_: E_.activation(ST[0:1, 512:1024], P[3][0:1, :], AF.Sigmoid))
        bfree(3, "as_", a_s)
        ds_s = dve(lambda E_: E_.tensor_scalar(SGBF[0:1, :], ST[0:1, :], 0.5, None,
                                               ALU.mult),
                   waits=[("as_", a_s)])
        dcm = None
        for tc in range(8):
            t_pe = tr(PT[:, tc * 8:(tc + 1) * 8], CTF[0:8, tc * 128:(tc + 1) * 128],
                      waits=[("ds", dl)])
            dcm = dve(lambda E_, s=PT[:, tc * 8:(tc + 1) * 8], d=CTM[:, tc * 8:(tc + 1) * 8]:
                      E_.tensor_copy(d, s), waits=[("pe", t_pe)])
            bfree(7, "ds", dcm)
        ds_prev = dcm
        for tc in range(8):
            sl = CTM[:, tc * 8:(tc + 1) * 8]
            st = CTM[:, 64 + tc * 8:64 + (tc + 1) * 8]
            d1 = dve(lambda E_, s=sl: E_.reduce_max(COL[0], s, axis=AX.X, negate=True),
                     waits=[("ds", ds_prev)])
            d6 = dve(lambda E_, s=sl, d=st: E_.tensor_scalar(d, s, COL[0], None, ALU.add),
                     waits=[("ds", d1)])
            d7 = dve(lambda E_, s=st, d=st: E_.tensor_scalar(d, s, 0.0, None, ALU.is_ge),
                     waits=[("ds", d6)])
            d8 = dve(lambda E_, s=st, d=st: E_.tensor_scalar(d, s, BIG, None, ALU.mult),
                     waits=[("ds", d7)])
            d9 = dve(lambda E_, a=sl, b=st, d=st: E_.tensor_tensor(d, a, b, ALU.subtract),
                     waits=[("ds", d8)])
            da = dve(lambda E_, s=st: E_.reduce_max(COL[1], s, axis=AX.X),
                     waits=[("ds", d9)])
            ab = act(lambda E_: E_.activation(COL[2], COL[1], AF.Exp, bias=COL[0]),
                     waits=[("ds", da)])
            db = dve(lambda E_: E_.tensor_scalar(COL[3], COL[2], 1.0, None, ALU.add),
                     waits=[("as_", ab)])
            dc = dve(lambda E_: E_.reciprocal(COL[4], COL[3]), waits=[("ds", db)])
            ac = act(lambda E_, s=sl, d=st: E_.activation(d, s, AF.Exp, bias=COL[0]),
                     waits=[("ds", dc)])
            dd = dve(lambda E_, s=sl, d=sl: E_.tensor_scalar(d, s, COL[1], None, ALU.is_ge),
                     waits=[("as_", ac)])
            de = dve(lambda E_, a=st, b=sl, d=st: E_.tensor_tensor(d, a, b, ALU.mult),
                     waits=[("ds", dd)])
            df = dve(lambda E_, s=st, d=C8_SCR[:, tc * 8:(tc + 1) * 8]:
                     E_.tensor_scalar(d, s, COL[4], None, ALU.mult),
                     waits=[("ds", de)])
            t_pe = tr(PT[0:8, 0:128], C8_SCR[:, tc * 8:(tc + 1) * 8], waits=[("ds", df)])
            ds_prev = dve(lambda E_, s=PT[0:8, 0:128],
                          d=CTF[0:8, tc * 128:(tc + 1) * 128]:
                          E_.tensor_copy(d, s), waits=[("pe", t_pe)])
            bfree(7, "ds", ds_prev)
        return ds_prev, ds_s

    def emit_cb(e, gate_ds):
        lro = []
        for tb in range(2):
            if e < E8:
                lro.append((OH05[:, e * 128:(e + 1) * 128],
                            CTF[0:8, tb * 512:(tb + 1) * 512], P[4 + tb][:, :], True, True))
            else:
                lro.append((ONESR[:, :], SGBF[0:1, tb * 512:(tb + 1) * 512],
                            P[4 + tb][:, :], True, True))
        peb = mmg(lro, waits=[("ds", gate_ds)] + CONST_W, banks=(4, 5))
        dc = None
        for tb in range(2):
            dc = dve(lambda E_, s=P[4 + tb][:, :], d=CB[:, tb * 512:(tb + 1) * 512]:
                     E_.tensor_copy(d, s), waits=[("pe", peb)])
            bfree(4 + tb, "ds", dc)
        return dc

    GU_Q = [(0, 384), (384, 384), (768, 384), (1152, 256)]
    D_Q = [(0, 512), (512, 512), (1024, 512), (1536, 512)]

    def declare_expert_pieces(e):
        if e < E8:
            wg_ap, wu_ap, wd_ap = wg_d.ap()[e], wu_d.ap()[e], wd_d.ap()[e]
        else:
            g = e - E8
            wg_ap = swg_d.ap()[:, g * II:(g + 1) * II]
            wu_ap = swu_d.ap()[:, g * II:(g + 1) * II]
            wd_ap = swd_d.ap()[g * II:(g + 1) * II, :]
        idx = {}
        for ph, wap, qs, cdim in (("g", wg_ap, GU_Q, NCH), ("u", wu_ap, GU_Q, NCH),
                                  ("d", wd_ap, D_Q, NIC)):
            for qi, (c0, cn) in enumerate(qs):
                src = wap.rearrange("(c p) i -> p c i", p=128)[:, :, c0:c0 + cn]
                idx[(ph, qi)] = (wq_declare(src, cdim * cn), c0, cn, cdim)
        return idx

    def emit_expert(e, xn_ds, cb_ds, pidx):
        def wv(ph, qi):
            j, c0, cn, cdim = pidx[(ph, qi)]
            slot, gw = wq_use(j)
            return (slot[:, 0:cdim * cn].rearrange("p (c i) -> p c i", c=cdim),
                    gw, j, c0)

        def gu_phase(ph, out_cb):
            last = None
            for ic in range(NIC):
                qi = min(ic * 128 // 384, 3)
                w, gw, j, c0 = wv(ph, qi)
                icol = ic * 128 - c0
                i0, i1 = (0, 1) if ic % 2 == 0 else (2, 3)
                b0, b1 = P[i0], P[i1]
                lro = []
                for hc in range(NCH):
                    l = w[:, hc, icol:icol + 128]
                    lro.append((l, xnv[:, hc, 0:512], b0[:, :], hc == 0, hc == NCH - 1))
                    lro.append((l, xnv[:, hc, 512:1024], b1[:, :], hc == 0, hc == NCH - 1))
                pe_c = mmg(lro, waits=[gw, ("ds", xn_ds)], banks=(i0, i1))
                if ic == NIC - 1 or min((ic + 1) * 128 // 384, 3) != qi:
                    wq_done(j)
                last = out_cb(ic, i0, i1, b0, b1, pe_c)
            return last

        def g_out(ic, i0, i1, b0, b1, pe_c):
            a0 = act(lambda E_, s=b0[:, :], d=sgcv(ic)[:, 0:512]:
                     E_.activation(d, s, AF.Silu), waits=[("pe", pe_c)])
            bfree(i0, "as_", a0)
            a1 = act(lambda E_, s=b1[:, :], d=sgcv(ic)[:, 512:1024]:
                     E_.activation(d, s, AF.Silu))
            bfree(i1, "as_", a1)
            return a1

        g_as = gu_phase("g", g_out)

        def u_out(ic, i0, i1, b0, b1, pe_c):
            c0_ = dve(lambda E_, s=b0[:, :]: E_.tensor_copy(UC_SCR[:, 0:512], s),
                      waits=[("pe", pe_c)])
            bfree(i0, "ds", c0_)
            c1_ = dve(lambda E_, s=b1[:, :]: E_.tensor_copy(UC_SCR[:, 512:1024], s))
            bfree(i1, "ds", c1_)
            m1 = dve(lambda E_, ic=ic: E_.tensor_tensor(SG_SCR[:, :], sgcv(ic), CB[:, :],
                                                        ALU.mult),
                     waits=[("ds", c1_), ("as_", g_as), ("ds", cb_ds)])
            return dve(lambda E_, ic=ic: E_.tensor_tensor(HB[:, ic * TF:(ic + 1) * TF],
                                                          SG_SCR[:, :], UC_SCR[:, :],
                                                          ALU.mult),
                       waits=[("ds", m1)])

        u_ds = gu_phase("u", u_out)
        xn_free_pe = pl.cnt("pe")

        last_acc = None
        ddt = {0: ("cs", 0), 1: ("cs", 0)}
        for hc2 in range(NCH):
            qi = hc2 * 128 // 512
            w, gw, j, c0 = wv("d", qi)
            hcol = hc2 * 128 - c0
            i0, i1 = (4, 5) if hc2 % 2 == 0 else (6, 0)
            b0, b1 = P[i0], P[i1]
            lro = []
            for ic in range(NIC):
                l = w[:, ic, hcol:hcol + 128]
                lro.append((l, HB[:, ic * TF:ic * TF + 512], b0[:, :],
                            ic == 0, ic == NIC - 1))
                lro.append((l, HB[:, ic * TF + 512:(ic + 1) * TF], b1[:, :],
                            ic == 0, ic == NIC - 1))
            pe_c = mmg(lro, waits=[gw, ("ds", u_ds)], banks=(i0, i1))
            if hc2 == NCH - 1 or (hc2 + 1) * 128 >= c0 + 512:
                wq_done(j)
            a0 = act(lambda E_, s=b0[:, :]: E_.copy(DT0, s),
                     waits=[("pe", pe_c), ddt[0]])
            bfree(i0, "as_", a0)
            a1 = act(lambda E_, s=b1[:, :]: E_.copy(DT1, s), waits=[ddt[1]])
            bfree(i1, "as_", a1)
            v0 = dve(lambda E_, hc2=hc2: E_.tensor_tensor(accv[:, hc2, 0:512],
                                                          accv[:, hc2, 0:512], DT0,
                                                          ALU.add),
                     waits=[("as_", a0)])
            ddt[0] = ("ds", v0)
            last_acc = dve(lambda E_, hc2=hc2: E_.tensor_tensor(accv[:, hc2, 512:1024],
                                                                accv[:, hc2, 512:1024],
                                                                DT1, ALU.add),
                           waits=[("as_", a1)])
            ddt[1] = ("ds", last_acc)
        return last_acc, xn_free_pe

    def emit_ffn_stage(xd, od, dep=()):
        s1 = None
        xn_free = None
        acc_free = ()
        for hf in range(2):
            tg0 = hf * TF
            xn_ds = emit_ln(xd, tg0, 0, dep, xn_free_pe=xn_free, acc_free=acc_free)
            gate_ds, sg_ds = emit_gating(xn_ds)
            pidx = [declare_expert_pieces(e) for e in range(12)]
            last = None
            for e in range(12):
                cb_ds = emit_cb(e, gate_ds if e < E8 else sg_ds)
                last, xn_free = emit_expert(e, xn_ds, cb_ds, pidx[e])
            s1 = S_AS.dma(fmv(od)[:, :, tg0:tg0 + TF], accv[:, :, :],
                          waits=[("ds", last)])
            acc_free = [s1]
        return s1

    def emit_attention(x1_dep):
        emit_ln(x1_d, 0, 0, x1_dep)
        xn2_ds = emit_ln(x1_d, TF, TF, x1_dep)

        BV = HB[:, 2048:4096]
        g_bv = S_G.dma(BV[0:1, :], bqkv_row_d.ap()[:, 2 * H:3 * H],
                       waits=[("ds", pl.cnt("ds"))])
        lro = [(ONESR[:, :], BV[0:1, b * 512:(b + 1) * 512], P[b][:, :], True, True)
               for b in range(4)]
        pebv = mmg(lro, waits=[g_bv] + CONST_W, banks=(0, 1, 2, 3))
        dbv = None
        for b in range(4):
            dbv = dve(lambda E_, b=b, s=P[b][:, :]:
                      E_.tensor_copy(BV[:, b * 512:(b + 1) * 512], s),
                      waits=[("pe", pebv)])
            bfree(b, "ds", dbv)

        wqv = wqkvT_d.ap().rearrange("(c p) o -> p c o", p=128)
        slotA_flat = ACCbf[:, 0:16384]
        slotB_flat = WS[:, 0:16384]
        QT = HB[:, 0:2048]
        qt_store = ("cs", 0)
        vt_store = {0: ("cs", 0), 1: ("cs", 0)}
        pe0 = pl.cnt("pe")
        slot_done = {"A": pe0, "B": pe0}
        cc_issued = {}

        def qk_piece(pi, sl):
            flat = slotA_flat if sl == "A" else slotB_flat
            nonlocal qt_store
            w = [("pe", slot_done[sl]), ("ds", xn2_ds)]
            if sl == "A":
                w.append(("ds", pl.cnt("ds")))
            gq_w = S_QW.dma(flat, wqv[:, :, pi * 1024:(pi + 1) * 1024], waits=w)
            slot = flat.rearrange("p (c o) -> p c o", c=NCH)
            for oi in range(8):
                oc = pi * 8 + oi
                lro = []
                for hc in range(NCH):
                    for tb in range(4):
                        lro.append((slot[:, hc, oi * 128:(oi + 1) * 128],
                                    xnv[:, hc, tb * 512:(tb + 1) * 512],
                                    P[tb][:, :], hc == 0, hc == NCH - 1))
                pe_c = mmg(lro, waits=[gq_w, ("ds", xn2_ds)], banks=(0, 1, 2, 3))
                alast = None
                for tb in range(4):
                    alast = act(lambda E_, tb=tb, oc=oc, s=P[tb][:, :]:
                                E_.activation(QT[:, tb * 512:(tb + 1) * 512], s,
                                              AF.Identity, bias=BQKV[:, oc:oc + 1]),
                                waits=[("pe", pe_c), qt_store] + CONST_W)
                    bfree(tb, "as_", alast)
                dst = q_dram if oc < 16 else cc_in_d
                row0 = (oc % 16) * 128
                qt_store = S_KS.dma(dst.ap()[row0:row0 + 128, :], QT[:, :],
                                    waits=[("as_", alast)])
            slot_done[sl] = pl.cnt("pe")

        def v_piece(pi):
            w = [("pe", slot_done["B"]), ("ds", xn2_ds)]
            gq_w = S_QW.dma(slotB_flat, wqv[:, :, pi * 1024:(pi + 1) * 1024], waits=w)
            slot = slotB_flat.rearrange("p (c o) -> p c o", c=NCH)
            VT = HB[:, 4096 + (pi - 4) * 1024: 4096 + (pi - 3) * 1024]
            for tc in range(NCH):
                pe_c = None
                for ob in range(2):
                    lro = []
                    for hc in range(NCH):
                        lro.append((xnv[:, hc, tc * 128:(tc + 1) * 128],
                                    slot[:, hc, ob * 512:(ob + 1) * 512],
                                    P[4 + ob][:, :], hc == 0, hc == NCH - 1))
                    pe_c = mmg(lro, waits=[gq_w, ("ds", xn2_ds)], banks=(4 + ob,))
                dl = None
                for ob in range(2):
                    c0_ = dve(lambda E_, ob=ob, s=P[4 + ob][:, :], vt=VT:
                              E_.tensor_copy(vt[:, ob * 512:(ob + 1) * 512], s),
                              waits=[("pe", pe_c), vt_store[pi - 4]])
                    bfree(4 + ob, "ds", c0_)
                    dl = dve(lambda E_, ob=ob, pi=pi, vt=VT:
                             E_.tensor_tensor(vt[:, ob * 512:(ob + 1) * 512],
                                              vt[:, ob * 512:(ob + 1) * 512],
                                              BV[:, (pi - 4) * 1024 + ob * 512:
                                                 (pi - 4) * 1024 + (ob + 1) * 512],
                                              ALU.add),
                             waits=[("ds", c0_), ("ds", dbv)])
                vt_store[pi - 4] = S_VS.dma(
                    cc_in_d.ap()[T + tc * 128:T + (tc + 1) * 128,
                                 (pi - 4) * 1024:(pi - 3) * 1024],
                    VT[:, :], waits=[("ds", dl)])
            slot_done["B"] = pl.cnt("pe")

        qk_piece(2, "A")
        qk_piece(3, "B")
        v_piece(4)
        v_piece(5)

        kv_waits = S_KS.all_waits() + S_VS.all_waits()
        cs1 = pl.op("gpsimd",
                    lambda E_: E_.collective_compute(
                        "AllGather", ALU.bypass,
                        replica_groups=[list(range(world))],
                        ins=[cc_in_d.ap()], outs=[cc_out_d.ap()]),
                    waits=kv_waits, inc="cs", inc_by=1)

        def sw_fn(E_):
            pid = nc.partition_id([mybir.EngineType.Pool])
            for i in nc.Switch(engines=[E_], index=[pid], n=8):
                p = (i ^ 1) % world
                E_.dma_start(out=kvp_d.ap(),
                             in_=cc_out_d.ap()[p * 2 * T:(p + 1) * 2 * T, :]
                             ).then_inc(sems["kv"], 16)
        pl.op("gpsimd", sw_fn, waits=[("cs", pl.cnt("cs"))])
        kvp_w = ("kv", pl.bump("kv", 16))

        qk_piece(0, "A")
        qk_piece(1, "B")

        QTILE = ACCbf[:, 0:8192].rearrange("p (c t) -> p c t", c=NCH)
        SCq = [ACCbf[:, 8192 + q * 4096:8192 + (q + 1) * 4096] for q in range(4)]
        X1C = [ACC[:, 12288 + i * 512:12288 + (i + 1) * 512] for i in range(2)]
        PR = XN[:, 0:4096]
        PRT = XN[:, 4096:20480].rearrange("p (k t) -> p k t", k=32)
        CTX = XN[:, 20480:28672].rearrange("p (c t) -> p c t", c=NCH)
        KT = [WS[:, 0:8192], WS[:, 8192:16384]]
        VT4 = [HB[:, 4096 + i * 3584:4096 + (i + 1) * 3584] for i in range(2)]

        qv = q_dram.ap().rearrange("(c p) t -> p c t", p=128)
        kv_own = cc_in_d.ap()[0:T, :].rearrange("(c p) t -> p c t", p=128)
        kv_par = kvp_d.ap()[0:T, :].rearrange("(c p) t -> p c t", p=128)
        x1v = fmv(x1_d)
        x2v = fmv(x2_d)
        x2_store = {}
        last_x2 = None
        pe_qkv = pl.cnt("pe")
        kt_done = {0: pe_qkv, 1: pe_qkv}
        vt4_done = {0: pe_qkv, 1: pe_qkv}
        scq_free = {qt: ("pe", pe_qkv) for qt in range(4)}
        qtile_done = pe_qkv
        pr_done = 0
        dt_free = {0: ("cs", 0), 1: ("cs", 0)}

        for qb in range(4):
            gq_q = S_Q.dma(ACCbf[:, 0:8192], qv[:, :, qb * 512:(qb + 1) * 512],
                           waits=[("pe", qtile_done)] +
                                 (S_KS.all_waits() if qb == 0 else []))
            for kb in range(8):
                khalf = kb // 4
                src2 = kv_own if khalf == 0 else kv_par
                kw = [("pe", kt_done[kb % 2])]
                if khalf == 1:
                    kw.append(kvp_w)
                else:
                    kw += S_KS.all_waits() + S_VS.all_waits()
                gk = S_K.dma(KT[kb % 2],
                             src2[:, :, (kb % 4) * 512:((kb % 4) + 1) * 512], waits=kw)
                ktv = KT[kb % 2].rearrange("p (c t) -> p c t", c=NCH)
                for qt in range(4):
                    bi = qt if kb % 2 == 0 else (qt + 4) % 7
                    lro = []
                    for hc in range(NCH):
                        lro.append((QTILE[:, hc, qt * 128:(qt + 1) * 128],
                                    ktv[:, hc, :], P[bi][:, :], hc == 0, hc == NCH - 1))
                    pe_c = mmg(lro, waits=[gk, gq_q], banks=(bi,))
                    cw = [("pe", pe_c)]
                    if kb == 0:
                        cw.append(scq_free[qt])
                    cc2 = dve(lambda E_, s=P[bi][:, :], qt=qt, kb=kb:
                              E_.tensor_copy(SCq[qt][:, kb * 512:(kb + 1) * 512], s),
                              waits=cw)
                    bfree(bi, "ds", cc2)
                kt_done[kb % 2] = pl.cnt("pe")
            qtile_done = pl.cnt("pe")
            for qt in range(4):
                d1 = dve(lambda E_, qt=qt: E_.reduce_max(COL[5], SCq[qt], axis=AX.X,
                                                         negate=True))
                d2 = dve(lambda E_: E_.tensor_scalar(COL[6], COL[5], ISCL, None, ALU.mult),
                         waits=[("ds", d1)])
                a1 = act(lambda E_, qt=qt: E_.activation(PR[:, :], SCq[qt], AF.Exp,
                                                         bias=COL[6], scale=ISCL,
                                                         accum_out=COL[7]),
                         waits=[("ds", d2), ("pe", pr_done)])
                scq_free[qt] = ("as_", a1)
                d3 = dve(lambda E_: E_.reciprocal(COL[5], COL[7]), waits=[("as_", a1)])
                d4 = dve(lambda E_: E_.tensor_scalar(PR[:, :], PR[:, :], COL[5], None,
                                                     ALU.mult),
                         waits=[("ds", d3)])
                for kg in range(4):
                    pe_t = None
                    for kk in range(8):
                        ktc = kg * 8 + kk
                        pe_t = tr(PT[:, kk * 128:(kk + 1) * 128],
                                  PR[:, ktc * 128:(ktc + 1) * 128],
                                  waits=[("ds", d4)], banks=(7,) if kk == 0 else ())
                    prc = dve(lambda E_, kg=kg, qt=qt,
                              s=PT[:, :].rearrange("p (k t) -> p k t", k=8):
                              E_.tensor_copy(PRT[:, kg * 8:(kg + 1) * 8,
                                                 qt * 128:(qt + 1) * 128], s),
                              waits=[("pe", pe_t)])
                    bfree(7, "ds", prc)
                pr_done = pl.cnt("pe")
            prt_ds = pl.cnt("ds")
            ctx_as = None
            for gg, (h0, hn) in enumerate(((0, 7), (7, 7), (14, 2))):
                pe_c = None
                for kt4 in range(8):
                    khalf = kt4 // 4
                    src_d = cc_in_d if khalf == 0 else kvp_d
                    r0 = T + (kt4 % 4) * 512
                    vw = [("pe", vt4_done[kt4 % 2])]
                    if khalf == 1:
                        vw.append(kvp_w)
                    else:
                        vw += S_VS.all_waits()
                    gv = S_V.dma(VT4[kt4 % 2][:, 0:4 * hn * 128],
                                 src_d.ap()[r0:r0 + 512, h0 * 128:(h0 + hn) * 128]
                                 .rearrange("(c p) h -> p c h", p=128),
                                 waits=vw)
                    vt = VT4[kt4 % 2][:, 0:4 * hn * 128].rearrange(
                        "p (c h) -> p c h", c=4)
                    for c4 in range(4):
                        ktc = kt4 * 4 + c4
                        lro = []
                        for hh in range(hn):
                            lro.append((vt[:, c4, hh * 128:(hh + 1) * 128],
                                        PRT[:, ktc, :], P[hh][:, :],
                                        ktc == 0, ktc == 31))
                        pe_c = mmg(lro, waits=[gv, ("ds", prt_ds)],
                                   banks=tuple(range(hn)) if ktc == 0 else ())
                    vt4_done[kt4 % 2] = pl.cnt("pe")
                for hh in range(hn):
                    ca = act(lambda E_, hh=hh, h0=h0, s=P[hh][:, :]:
                             E_.copy(CTX[:, h0 + hh, :], s),
                             waits=[("pe", pe_c)])
                    bfree(hh, "as_", ca)
                    ctx_as = ca
            for pi in range(4):
                gw_w = S_QW.dma(KT[pi % 2][:, 0:8192],
                                woutT_d.ap().rearrange("(c p) o -> p c o", p=128)
                                [:, :, pi * 512:(pi + 1) * 512],
                                waits=[("pe", kt_done[pi % 2])])
                wot = KT[pi % 2].rearrange("p (c o) -> p c o", c=NCH)
                for oi in range(4):
                    oc = pi * 4 + oi
                    bi = oc % 2
                    lro = []
                    for hc in range(NCH):
                        lro.append((wot[:, hc, oi * 128:(oi + 1) * 128],
                                    CTX[:, hc, :], P[bi][:, :], hc == 0, hc == NCH - 1))
                    pe_c = mmg(lro, waits=[gw_w, ("as_", ctx_as)], banks=(bi,))
                    dtt = DT0 if bi == 0 else DT1
                    ao = act(lambda E_, s=P[bi][:, :], oc=oc, d=dtt:
                             E_.activation(d, s, AF.Identity, bias=BOUT[:, oc:oc + 1]),
                             waits=[("pe", pe_c), dt_free[bi]] + CONST_W)
                    bfree(bi, "as_", ao)
                    gx = S_XL.dma(X1C[oc % 2], x1v[:, oc, qb * 512:(qb + 1) * 512],
                                  waits=[x2_store.get(oc % 2, ("cs", 0))])
                    dx = dve(lambda E_, oc=oc, d=dtt: E_.tensor_tensor(
                        X1C[oc % 2], X1C[oc % 2], d, ALU.add),
                             waits=[("as_", ao), gx])
                    dt_free[bi] = ("ds", dx)
                    sx = S_XS.dma(x2v[:, oc, qb * 512:(qb + 1) * 512],
                                  X1C[oc % 2], waits=[("ds", dx)])
                    x2_store[oc % 2] = sx
                    last_x2 = sx
                kt_done[pi % 2] = pl.cnt("pe")
        return last_x2

    if upto == "ffn1":
        emit_ffn_stage(x0_d, y_d)
    elif upto == "attn":
        emit_ffn_stage(x0_d, x1_d)
        x1w = S_AS.all_waits()
        emit_attention(x1w)
        S_AS.dma(y_d.ap(), x2_d.ap(), waits=S_XS.all_waits())
    else:
        emit_ffn_stage(x0_d, x1_d)
        x1w = S_AS.all_waits()
        emit_attention(x1w)
        wq["floor"] = pl.cnt("pe")
        emit_ffn_stage(x2_d, y_d, dep=S_XS.all_waits())

    if dbg:
        if upto != "ffn1":
            dbg_cco = nc.dram_tensor("dbg_cco", [world * 2 * T, H], BF16,
                                     kind="ExternalOutput")
            dbg_kvp = nc.dram_tensor("dbg_kvp", [2 * T, H], BF16, kind="ExternalOutput")
            dbg_q = nc.dram_tensor("dbg_q", [H, T], BF16, kind="ExternalOutput")
            dbg_cci = nc.dram_tensor("dbg_cci", [2 * T, H], BF16, kind="ExternalOutput")
            dwx = [("pe", pl.cnt("pe")), ("ds", pl.cnt("ds")), ("as_", pl.cnt("as_"))]
            S_AS.dma(dbg_cco.ap(), cc_out_d.ap(), waits=dwx)
            S_AS.dma(dbg_kvp.ap(), kvp_d.ap(), waits=dwx)
            S_AS.dma(dbg_q.ap(), q_dram.ap(), waits=dwx)
            S_AS.dma(dbg_cci.ap(), cc_in_d.ap(), waits=dwx)
        dbg_xn = nc.dram_tensor("dbg_xn", [128, NCH * TF], BF16, kind="ExternalOutput")
        dbg_cb = nc.dram_tensor("dbg_cb", [128, 1024], BF16, kind="ExternalOutput")
        dbg_hb = nc.dram_tensor("dbg_hb", [128, NIC * TF], BF16, kind="ExternalOutput")

        dw = [("pe", pl.cnt("pe")), ("ds", pl.cnt("ds")), ("as_", pl.cnt("as_"))]
        S_AS.dma(dbg_xn.ap().rearrange("p (c t) -> p c t", t=TF), xnv[:, :, 0:TF],
                 waits=dw)
        S_AS.dma(dbg_cb.ap(), CB[:, :], waits=dw)
        S_AS.dma(dbg_hb.ap(), HB[:, :], waits=dw)


    fin = S_AS.all_waits() + S_XS.all_waits()
    pl.op("scalar", lambda E_: E_.nop(), waits=fin)

    with nc.Block() as block:
        def mk(name):
            def _(E_):
                for fn, wl, inc, inc_by in pl.ops[name]:
                    for s, v in wl:
                        E_.wait_ge(sems[s], v)
                    r = fn(E_)
                    if inc is not None and r is not None:
                        r.then_inc(sems[inc], inc_by)
            return _
        block.sync(mk("sync"))
        block.gpsimd(mk("gpsimd"))
        block.tensor(mk("tensor"))
        block.vector(mk("vector"))
        block.scalar(mk("scalar"))
    ctx.close()
    return nc


_NC_CACHE = {}


def kernel(L0_attn_hidden_states, L0_attn_w_qkv, L0_attn_b_qkv, L0_attn_w_out,
           L0_attn_b_out, L0_ffn_gate_w, L0_ffn_w_gate, L0_ffn_w_up, L0_ffn_w_down,
           L0_ffn_sw_gate, L0_ffn_sw_up, L0_ffn_sw_down, L0_ffn_shared_gate_w):
    x0 = np.asarray(L0_attn_hidden_states, dtype=np.float32)
    w_qkv = np.asarray(L0_attn_w_qkv, np.float32)
    b_qkv = np.asarray(L0_attn_b_qkv, np.float32)
    w_out = np.asarray(L0_attn_w_out, np.float32)
    b_out = np.asarray(L0_attn_b_out, np.float32)
    gate_w = np.asarray(L0_ffn_gate_w, np.float32)

    shared = {
        "wqkvT": np.ascontiguousarray(w_qkv.T),
        "bqkv_pc": np.ascontiguousarray(b_qkv.reshape(48, 128).T),
        "bqkv_row": np.ascontiguousarray(b_qkv.reshape(1, 3 * H)),
        "woutT": np.ascontiguousarray(w_out.T),
        "bout_pc": np.ascontiguousarray(b_out.reshape(16, 128).T),
        "gatew_pc": np.ascontiguousarray(
            gate_w.T.reshape(16, 128, 8).transpose(1, 0, 2).reshape(128, 128)),
        "sgw_pc": np.ascontiguousarray(
            np.asarray(L0_ffn_shared_gate_w, np.float32).reshape(16, 128).T),
        "wg": np.ascontiguousarray(np.asarray(L0_ffn_w_gate, np.float32)),
        "wu": np.ascontiguousarray(np.asarray(L0_ffn_w_up, np.float32)),
        "wd": np.ascontiguousarray(np.asarray(L0_ffn_w_down, np.float32)),
        "swg": np.ascontiguousarray(np.asarray(L0_ffn_sw_gate, np.float32)),
        "swu": np.ascontiguousarray(np.asarray(L0_ffn_sw_up, np.float32)),
        "swd": np.ascontiguousarray(np.asarray(L0_ffn_sw_down, np.float32)),
    }
    in_maps = []
    for c in range(8):
        b, hf = c // 2, c % 2
        m = dict(shared)
        m["x0"] = np.ascontiguousarray(x0[b, hf * T:(hf + 1) * T, :].T)
        in_maps.append(m)

    if "nc" not in _NC_CACHE:
        _NC_CACHE["nc"] = build_program("full")
    res = run_bass_kernel_spmd(_NC_CACHE["nc"], in_maps, core_ids=list(range(8)))
    out = np.empty((4, 2 * T, H), np.float32)
    for c in range(8):
        b, hf = c // 2, c % 2
        out[b, hf * T:(hf + 1) * T, :] = np.asarray(res.results[c]["y"]).T
    return out


# version 04
# speedup vs baseline: 1.6734x; 1.1326x over previous
import math
import numpy as np
import concourse.bass as bass
import concourse.mybir as mybir
from concourse.bass_utils import run_bass_kernel_spmd

F32 = mybir.dt.float32
BF16 = mybir.dt.bfloat16
AF = mybir.ActivationFunctionType
ALU = mybir.AluOpType
AX = mybir.AxisListType

H = 2048
NCH = 16
T = 2048
TF = 1024
E8 = 8
II = 1408
NIC = 11
EPS = 1e-5
ISCL = 1.0 / math.sqrt(2048.0)
BIG = 1.0e30

ENGS = ("sync", "gpsimd", "tensor", "vector", "scalar")


class Plan:
    def __init__(self):
        self.ops = {e: [] for e in ENGS}
        self.sems = {}
        self.floors = {e: {} for e in ENGS}

    def op(self, eng, fn, waits=(), inc=None, inc_by=1):
        wl = []
        for s, v in waits:
            if v is None or v <= 0:
                continue
            if self.floors[eng].get(s, 0) < v:
                self.floors[eng][s] = v
                wl.append((s, v))
        if inc is not None:
            self.sems[inc] = self.sems.get(inc, 0) + inc_by
        self.ops[eng].append((fn, wl, inc, inc_by))
        return self.sems.get(inc) if inc is not None else None

    def bump(self, s, n):
        self.sems[s] = self.sems.get(s, 0) + n
        return self.sems[s]

    def cnt(self, s):
        return self.sems.get(s, 0)


def _bf(a):
    import ml_dtypes
    return np.ascontiguousarray(a).astype(ml_dtypes.bfloat16)


def build_program(upto="full", dbg=False, world=8):
    nc = bass.Bass("TRN2", target_bir_lowering=False)

    x0_d = nc.dram_tensor("x0", [H, T], F32, kind="ExternalInput")
    wqkvT_d = nc.dram_tensor("wqkvT", [H, 3 * H], F32, kind="ExternalInput")
    bqkv_pc_d = nc.dram_tensor("bqkv_pc", [128, 48], F32, kind="ExternalInput")
    bqkv_row_d = nc.dram_tensor("bqkv_row", [1, 3 * H], F32, kind="ExternalInput")
    woutT_d = nc.dram_tensor("woutT", [H, H], F32, kind="ExternalInput")
    bout_pc_d = nc.dram_tensor("bout_pc", [128, 16], F32, kind="ExternalInput")
    gatewT_d = nc.dram_tensor("gatew_pc", [128, NCH * E8], F32, kind="ExternalInput")
    wg_d = nc.dram_tensor("wg", [E8, H, II], F32, kind="ExternalInput")
    wu_d = nc.dram_tensor("wu", [E8, H, II], F32, kind="ExternalInput")
    wd_d = nc.dram_tensor("wd", [E8, II, H], F32, kind="ExternalInput")
    swg_d = nc.dram_tensor("swg", [H, 4 * II], F32, kind="ExternalInput")
    swu_d = nc.dram_tensor("swu", [H, 4 * II], F32, kind="ExternalInput")
    swd_d = nc.dram_tensor("swd", [4 * II, H], F32, kind="ExternalInput")
    sgw_d = nc.dram_tensor("sgw_pc", [128, NCH], F32, kind="ExternalInput")

    y_d = nc.dram_tensor("y", [H, T], F32, kind="ExternalOutput")

    x1_d = nc.dram_tensor("x1buf", [H, T], F32, kind="Internal")
    x2_d = nc.dram_tensor("x2buf", [H, T], F32, kind="Internal")
    q_dram = nc.dram_tensor("qbuf", [H, T], BF16, kind="Internal")
    cc_in_d = nc.dram_tensor("cc_in", [2 * T, H], BF16, kind="Internal")
    cc_out_d = nc.dram_tensor("cc_out", [2 * 2 * T, H], BF16, kind="Internal")

    ident_c = nc.inline_tensor(_bf(np.eye(128, dtype=np.float32)), name="ident_c")
    onescol_c = nc.inline_tensor(_bf(np.ones((128, 1), np.float32)), name="onescol_c")
    onesrow_c = nc.inline_tensor(_bf(np.ones((1, 128), np.float32)), name="onesrow_c")
    onesrowf_c = nc.inline_tensor(np.ones((128, 128), np.float32), name="onesrowf_c")
    oh = np.zeros((8, 8 * 128), np.float32)
    for e in range(8):
        oh[e, e * 128:(e + 1) * 128] = 0.5
    oh05_c = nc.inline_tensor(_bf(oh), name="oh05_c")

    import contextlib
    ctx = contextlib.ExitStack()
    sb = lambda n, s, d: ctx.enter_context(nc.sbuf_tensor(n, s, d))
    psa = lambda n, s, d: ctx.enter_context(nc.psum_tensor(n, s, d))
    mksem = lambda n: ctx.enter_context(nc.semaphore(n))

    XN = sb("XN", [128, NCH * T], BF16)
    ACC = sb("ACC", [128, NCH * TF], F32)
    WS = sb("WS", [128, 3 * 6144], BF16)
    HB = sb("HB", [128, NIC * TF], BF16)
    MI = sb("MI", [128, 4096], F32)
    CB = sb("CB", [128, 1024], BF16)
    OH05 = sb("OH05", [8, 8 * 128], BF16)
    IDENT = sb("IDENT", [128, 128], BF16)
    ONESC = sb("ONESC", [128, 1], BF16)
    ONESR = sb("ONESR", [1, 128], BF16)
    ONESRF = sb("ONESRF", [128, 128], F32)
    GWT = sb("GWT", [128, NCH * E8], BF16)
    SGWT = sb("SGWT", [128, NCH], BF16)
    BQKV = sb("BQKV", [128, 48], F32)
    BOUT = sb("BOUT", [128, 16], F32)

    P = [psa("P%d" % i, [128, 512], F32) for i in range(7)]
    PT = psa("PT", [128, 1024], BF16)

    SEMNAMES = ("pe", "ds", "as_", "cs",
                "w0", "w1", "w2", "gm0", "gm1", "qw0", "qw1", "q2a", "q2b",
                "sp0", "sp1", "k0", "k1", "v0", "v1", "q0",
                "xl0", "xl1", "as0", "as1", "ks0", "ks1",
                "vs0", "vs1", "xs0", "xs1")
    sems = {s: mksem(s) for s in SEMNAMES}

    pl = Plan()

    xnv = XN[:, :].rearrange("p (c t) -> p c t", t=T)
    accv = ACC[:, :].rearrange("p (c t) -> p c t", t=TF)
    ACCbf = ACC[:, :].bitcast(BF16)
    MIbf = MI[:, :].bitcast(BF16)
    wslot = [WS[:, i * 6144:(i + 1) * 6144] for i in range(3)]
    ST = MI[:, 0:1024]
    ST2 = MI[:, 1024:2048]
    A_t = MI[:, 2048:3072]
    B_t = MI[:, 3072:4096]
    DT0 = MI[:, 2048:2560]
    DT1 = MI[:, 2560:3072]
    DT2 = MI[:, 3072:3584]
    COL = [MI[:, 3584 + k:3584 + k + 1] for k in range(8)]
    SGBF = MIbf[0:1, 6144:7168]
    sgcv = lambda ic: XN[:, ic * T + TF: ic * T + 2 * TF]
    SQ_SCR2 = [HB[:, 0:1024], HB[:, 1024:2048]]
    SG_SCR = XN[:, 12 * T + TF: 12 * T + 2 * TF]
    UC_SCR = XN[:, 13 * T + TF: 13 * T + 2 * TF]
    C8_SCR = XN[:, 14 * T + TF: 14 * T + TF + 512]
    CTF = XN[0:8, 15 * T + TF: 15 * T + 2 * TF]
    CTM = MI[:, 2048:2176]

    def dma_raw(eng, out, in_, waits=(), inc=None):
        return pl.op(eng, lambda E_, o=out, i=in_: E_.dma_start(out=o, in_=i),
                     waits=waits, inc=inc, inc_by=16)

    class Stream:

        def __init__(self, eng, names):
            self.eng, self.names = eng, names
            self.i = 0

        def dma(self, out, in_, waits=()):
            s = self.names[self.i % len(self.names)]
            self.i += 1
            w = list(waits) + [(s, pl.cnt(s))]
            c = dma_raw(self.eng, out, in_, waits=w, inc=s)
            return (s, c)

        def all_waits(self):
            return [(s, pl.cnt(s)) for s in self.names]

    S_G = Stream("gpsimd", ["gm0", "gm1"])
    S_QW = Stream("gpsimd", ["qw0", "qw1"])
    S_QW2 = Stream("sync", ["q2a", "q2b"])
    S_SP = Stream("sync", ["sp0", "sp1"])
    S_K = Stream("sync", ["k0", "k1"])
    S_V = Stream("sync", ["v0", "v1"])
    S_Q = Stream("sync", ["q0"])
    S_XL = Stream("sync", ["xl0", "xl1"])
    S_AS = Stream("scalar", ["as0", "as1"])
    S_KS = Stream("scalar", ["ks0", "ks1"])
    S_VS = Stream("scalar", ["vs0", "vs1"])
    S_XS = Stream("scalar", ["xs0", "xs1"])

    bankdep = {k: ("cs", 0) for k in range(8)}

    def bfree(k, sem, cnt):
        bankdep[k] = (sem, cnt)

    def mmg(lro, waits=(), banks=()):
        w = list(waits) + [bankdep[b] for b in banks]

        def fn(E_, lro=tuple(lro)):
            r = None
            for (l, rh, o, st, sp) in lro:
                r = E_.matmul(o, l, rh, start=st, stop=sp)
            return r
        return pl.op("tensor", fn, waits=w, inc="pe")

    def tr(out, in_, waits=(), banks=(7,)):
        w = list(waits) + [bankdep[b] for b in banks]
        p = in_.shape[0]
        return pl.op("tensor",
                     lambda E_, o=out, i=in_, p=p: E_.transpose(o, i, IDENT[0:p, 0:p]),
                     waits=w, inc="pe")

    def act(fn, waits=()):
        w = list(waits) + [("as_", pl.cnt("as_"))]
        return pl.op("scalar", fn, waits=w, inc="as_")

    def dve(fn, waits=()):
        w = list(waits) + [("ds", pl.cnt("ds"))]
        return pl.op("vector", fn, waits=w, inc="ds")

    def fmv(d):
        return d.ap().rearrange("(c p) t -> p c t", p=128)

    S_SP.dma(IDENT[:, :], ident_c.ap())
    S_SP.dma(ONESC[:, :], onescol_c.ap())
    S_SP.dma(ONESR[:, :], onesrow_c.ap())
    S_SP.dma(ONESRF[:, :], onesrowf_c.ap())
    S_SP.dma(OH05[:, :], oh05_c.ap())
    S_SP.dma(BQKV[:, :], bqkv_pc_d.ap())
    S_SP.dma(BOUT[:, :], bout_pc_d.ap())
    S_G.dma(GWT[:, :], gatewT_d.ap())
    S_G.dma(SGWT[:, :], sgw_d.ap())
    CONST_W = S_SP.all_waits()
    CONST_GW = S_G.all_waits()

    WQAHEAD = 2
    wq = {"pend": [], "issued": 0, "wof": {}, "done_pe": {}, "floor": 0}

    def wq_declare(src_ap, ncols):
        wq["pend"].append((src_ap, ncols))
        return len(wq["pend"]) - 1

    def wq_pump(upto_idx):
        while wq["issued"] <= min(upto_idx, len(wq["pend"]) - 1):
            j = wq["issued"]
            src, ncols = wq["pend"][j]
            w = [("pe", max(wq["done_pe"].get(j - 3, 0), wq["floor"]))]
            s = "w%d" % (j % 3)
            w.append((s, pl.cnt(s)))
            c = dma_raw("gpsimd", wslot[j % 3][:, 0:ncols], src, waits=w, inc=s)
            wq["wof"][j] = (s, c)
            wq["issued"] = j + 1

    def wq_use(j):
        wq_pump(j + WQAHEAD)
        assert j < wq["issued"]
        return wslot[j % 3], wq["wof"][j]

    def wq_done(j):
        wq["done_pe"][j] = pl.cnt("pe")

    def emit_ln(xd, tg0, xoff=0, dep=(), xn_free_pe=None, acc_free=()):
        xv = fmv(xd)
        if xn_free_pe is None:
            xn_free_pe = pl.cnt("pe")
        ldw = list(dep) + [("pe", xn_free_pe), ("ds", pl.cnt("ds"))]
        g0p = [S_G.dma(xnv[:, 4 * q:4 * (q + 1), xoff:xoff + TF],
                       xv[:, 4 * q:4 * (q + 1), tg0:tg0 + TF], waits=ldw)
               for q in range(4)]
        s0 = S_SP.dma(accv[:, :, :], xv[:, :, tg0:tg0 + TF],
                      waits=list(dep) + list(acc_free) + [("ds", pl.cnt("ds"))] +
                            S_AS.all_waits())
        pe_stats = None
        for hc in range(NCH):
            scr = SQ_SCR2[hc % 2]
            sq_ds = dve(lambda E_, s=xnv[:, hc, xoff:xoff + TF], d=scr:
                        E_.tensor_tensor(d, s, s, ALU.mult),
                        waits=[g0p[hc // 4], ("pe", pe_stats or xn_free_pe)])
            lro = []
            for b in range(2):
                lro.append((ONESC[:, :], xnv[:, hc, xoff + b * 512:xoff + (b + 1) * 512],
                            P[b][0:1, :], hc == 0, hc == NCH - 1))
            for b in range(2):
                lro.append((ONESC[:, :], scr[:, b * 512:(b + 1) * 512],
                            P[2 + b][0:1, :], hc == 0, hc == NCH - 1))
            pe_stats = mmg(lro, waits=[g0p[hc // 4], ("ds", sq_ds)] + CONST_W,
                           banks=(0, 1, 2, 3) if hc == 0 else ())
        d0 = None
        for b in range(2):
            c1 = dve(lambda E_, s=P[b][0:1, :], d=ST[0:1, b * 512:(b + 1) * 512]:
                     E_.tensor_copy(d, s), waits=[("pe", pe_stats)])
            bfree(b, "ds", c1)
            d0 = dve(lambda E_, s=P[2 + b][0:1, :], d=ST2[0:1, b * 512:(b + 1) * 512]:
                     E_.tensor_copy(d, s))
            bfree(2 + b, "ds", d0)
        inv = 1.0 / float(H)
        d1 = dve(lambda E_: E_.tensor_scalar(ST[0:1, :], ST[0:1, :], inv, None, ALU.mult),
                 waits=[("ds", d0)])
        d2 = dve(lambda E_: E_.tensor_tensor(A_t[0:1, :], ST[0:1, :], ST[0:1, :], ALU.mult),
                 waits=[("ds", d1)])
        d3 = dve(lambda E_: E_.tensor_scalar(ST2[0:1, :], ST2[0:1, :], inv, None, ALU.mult),
                 waits=[("ds", d2)])
        d4 = dve(lambda E_: E_.tensor_tensor(ST2[0:1, :], ST2[0:1, :], A_t[0:1, :],
                                             ALU.subtract),
                 waits=[("ds", d3)])
        d4b = dve(lambda E_: E_.tensor_scalar(ST2[0:1, :], ST2[0:1, :], EPS, None,
                                              ALU.add),
                  waits=[("ds", d4)])
        a1 = act(lambda E_: E_.activation(B_t[0:1, :], ST2[0:1, :], AF.Sqrt),
                 waits=[("ds", d4b)])
        d5 = dve(lambda E_: E_.reciprocal(A_t[0:1, :], B_t[0:1, :]), waits=[("as_", a1)])
        d6 = dve(lambda E_: E_.tensor_tensor(B_t[0:1, :], ST[0:1, :], A_t[0:1, :],
                                             ALU.mult),
                 waits=[("ds", d5)])
        lro = []
        for b in range(2):
            lro.append((ONESRF[0:1, :], A_t[0:1, b * 512:(b + 1) * 512], P[b][:, :],
                        True, True))
            lro.append((ONESRF[0:1, :], B_t[0:1, b * 512:(b + 1) * 512], P[2 + b][:, :],
                        True, True))
        peb = mmg(lro, waits=[("ds", d6)], banks=(0, 1, 2, 3))
        dab = None
        for b in range(2):
            c1 = dve(lambda E_, s=P[b][:, :], d=A_t[:, b * 512:(b + 1) * 512]:
                     E_.tensor_copy(d, s), waits=[("pe", peb)])
            bfree(b, "ds", c1)
            dab = dve(lambda E_, s=P[2 + b][:, :], d=B_t[:, b * 512:(b + 1) * 512]:
                      E_.tensor_copy(d, s))
            bfree(2 + b, "ds", dab)
        APS = ST[:, 0:512]
        dlast = None
        for hc in range(NCH):
            for b in range(2):
                t0 = dve(lambda E_, xc=accv[:, hc, b * 512:(b + 1) * 512],
                         a=A_t[:, b * 512:(b + 1) * 512]:
                         E_.tensor_tensor(APS, xc, a, ALU.mult),
                         waits=[s0, ("ds", dab), ("pe", pe_stats)])
                dlast = dve(lambda E_, bb=B_t[:, b * 512:(b + 1) * 512],
                            d=xnv[:, hc, xoff + b * 512:xoff + (b + 1) * 512]:
                            E_.tensor_tensor(d, APS, bb, ALU.subtract),
                            waits=[("ds", t0)])
        return dlast

    def emit_gating(xn_ds):
        gwv = GWT[:, :].rearrange("p (c e) -> p c e", c=NCH)
        pe_l = None
        for tb in range(2):
            lro = []
            for hc in range(NCH):
                lro.append((gwv[:, hc, :], xnv[:, hc, tb * 512:(tb + 1) * 512],
                            P[4 + tb][0:8, :], hc == 0, hc == NCH - 1))
            pe_l = mmg(lro, waits=[("ds", xn_ds)] + CONST_GW, banks=(4 + tb,))
        pe_s = None
        for tb, bank, bi in ((0, P[6], 6), (1, P[3], 3)):
            lro = []
            for hc in range(NCH):
                lro.append((SGWT[:, hc:hc + 1], xnv[:, hc, tb * 512:(tb + 1) * 512],
                            bank[0:1, :], hc == 0, hc == NCH - 1))
            pe_s = mmg(lro, waits=[("ds", xn_ds)] + CONST_GW, banks=(bi,))
        dl = None
        for tb, bank in ((0, P[4]), (1, P[5])):
            dl = dve(lambda E_, s=bank[0:8, :], d=CTF[0:8, tb * 512:(tb + 1) * 512]:
                     E_.tensor_copy(d, s), waits=[("pe", pe_l)])
            bfree(4 + tb, "ds", dl)
        a_s = act(lambda E_: E_.activation(ST[0:1, 0:512], P[6][0:1, :], AF.Sigmoid),
                  waits=[("pe", pe_s)])
        bfree(6, "as_", a_s)
        a_s = act(lambda E_: E_.activation(ST[0:1, 512:1024], P[3][0:1, :], AF.Sigmoid))
        bfree(3, "as_", a_s)
        ds_s = dve(lambda E_: E_.tensor_scalar(SGBF[0:1, :], ST[0:1, :], 0.5, None,
                                               ALU.mult),
                   waits=[("as_", a_s)])
        dcm = None
        for tc in range(8):
            t_pe = tr(PT[:, tc * 8:(tc + 1) * 8], CTF[0:8, tc * 128:(tc + 1) * 128],
                      waits=[("ds", dl)])
            dcm = dve(lambda E_, s=PT[:, tc * 8:(tc + 1) * 8], d=CTM[:, tc * 8:(tc + 1) * 8]:
                      E_.tensor_copy(d, s), waits=[("pe", t_pe)])
            bfree(7, "ds", dcm)
        ds_prev = dcm
        for tc in range(8):
            sl = CTM[:, tc * 8:(tc + 1) * 8]
            st = CTM[:, 64 + tc * 8:64 + (tc + 1) * 8]
            d1 = dve(lambda E_, s=sl: E_.reduce_max(COL[0], s, axis=AX.X, negate=True),
                     waits=[("ds", ds_prev)])
            d6 = dve(lambda E_, s=sl, d=st: E_.tensor_scalar(d, s, COL[0], None, ALU.add),
                     waits=[("ds", d1)])
            d7 = dve(lambda E_, s=st, d=st: E_.tensor_scalar(d, s, 0.0, None, ALU.is_ge),
                     waits=[("ds", d6)])
            d8 = dve(lambda E_, s=st, d=st: E_.tensor_scalar(d, s, BIG, None, ALU.mult),
                     waits=[("ds", d7)])
            d9 = dve(lambda E_, a=sl, b=st, d=st: E_.tensor_tensor(d, a, b, ALU.subtract),
                     waits=[("ds", d8)])
            da = dve(lambda E_, s=st: E_.reduce_max(COL[1], s, axis=AX.X),
                     waits=[("ds", d9)])
            ab = act(lambda E_: E_.activation(COL[2], COL[1], AF.Exp, bias=COL[0]),
                     waits=[("ds", da)])
            db = dve(lambda E_: E_.tensor_scalar(COL[3], COL[2], 1.0, None, ALU.add),
                     waits=[("as_", ab)])
            dc = dve(lambda E_: E_.reciprocal(COL[4], COL[3]), waits=[("ds", db)])
            ac = act(lambda E_, s=sl, d=st: E_.activation(d, s, AF.Exp, bias=COL[0]),
                     waits=[("ds", dc)])
            dd = dve(lambda E_, s=sl, d=sl: E_.tensor_scalar(d, s, COL[1], None, ALU.is_ge),
                     waits=[("as_", ac)])
            de = dve(lambda E_, a=st, b=sl, d=st: E_.tensor_tensor(d, a, b, ALU.mult),
                     waits=[("ds", dd)])
            df = dve(lambda E_, s=st, d=C8_SCR[:, tc * 8:(tc + 1) * 8]:
                     E_.tensor_scalar(d, s, COL[4], None, ALU.mult),
                     waits=[("ds", de)])
            t_pe = tr(PT[0:8, 0:128], C8_SCR[:, tc * 8:(tc + 1) * 8], waits=[("ds", df)])
            ds_prev = dve(lambda E_, s=PT[0:8, 0:128],
                          d=CTF[0:8, tc * 128:(tc + 1) * 128]:
                          E_.tensor_copy(d, s), waits=[("pe", t_pe)])
            bfree(7, "ds", ds_prev)
        return ds_prev, ds_s

    def emit_cb(e, gate_ds):
        lro = []
        for tb in range(2):
            if e < E8:
                lro.append((OH05[:, e * 128:(e + 1) * 128],
                            CTF[0:8, tb * 512:(tb + 1) * 512], P[4 + tb][:, :], True, True))
            else:
                lro.append((ONESR[:, :], SGBF[0:1, tb * 512:(tb + 1) * 512],
                            P[4 + tb][:, :], True, True))
        peb = mmg(lro, waits=[("ds", gate_ds)] + CONST_W, banks=(4, 5))
        dc = None
        for tb in range(2):
            dc = dve(lambda E_, s=P[4 + tb][:, :], d=CB[:, tb * 512:(tb + 1) * 512]:
                     E_.tensor_copy(d, s), waits=[("pe", peb)])
            bfree(4 + tb, "ds", dc)
        return dc

    GU_Q = [(0, 384), (384, 384), (768, 384), (1152, 256)]
    D_Q = [(0, 512), (512, 512), (1024, 512), (1536, 512)]

    def declare_expert_pieces(e):
        if e < E8:
            wg_ap, wu_ap, wd_ap = wg_d.ap()[e], wu_d.ap()[e], wd_d.ap()[e]
        else:
            g = e - E8
            wg_ap = swg_d.ap()[:, g * II:(g + 1) * II]
            wu_ap = swu_d.ap()[:, g * II:(g + 1) * II]
            wd_ap = swd_d.ap()[g * II:(g + 1) * II, :]
        idx = {}
        for ph, wap, qs, cdim in (("g", wg_ap, GU_Q, NCH), ("u", wu_ap, GU_Q, NCH),
                                  ("d", wd_ap, D_Q, NIC)):
            for qi, (c0, cn) in enumerate(qs):
                src = wap.rearrange("(c p) i -> p c i", p=128)[:, :, c0:c0 + cn]
                idx[(ph, qi)] = (wq_declare(src, cdim * cn), c0, cn, cdim)
        return idx

    def emit_expert(e, xn_ds, cb_ds, pidx):
        def wv(ph, qi):
            j, c0, cn, cdim = pidx[(ph, qi)]
            slot, gw = wq_use(j)
            return (slot[:, 0:cdim * cn].rearrange("p (c i) -> p c i", c=cdim),
                    gw, j, c0)

        def gu_phase(ph, out_cb):
            last = None
            for ic in range(NIC):
                qi = min(ic * 128 // 384, 3)
                w, gw, j, c0 = wv(ph, qi)
                icol = ic * 128 - c0
                i0, i1 = (0, 1) if ic % 2 == 0 else (2, 3)
                b0, b1 = P[i0], P[i1]
                lro = []
                for hc in range(NCH):
                    l = w[:, hc, icol:icol + 128]
                    lro.append((l, xnv[:, hc, 0:512], b0[:, :], hc == 0, hc == NCH - 1))
                    lro.append((l, xnv[:, hc, 512:1024], b1[:, :], hc == 0, hc == NCH - 1))
                pe_c = mmg(lro, waits=[gw, ("ds", xn_ds)], banks=(i0, i1))
                if ic == NIC - 1 or min((ic + 1) * 128 // 384, 3) != qi:
                    wq_done(j)
                last = out_cb(ic, i0, i1, b0, b1, pe_c)
            return last

        def g_out(ic, i0, i1, b0, b1, pe_c):
            a0 = act(lambda E_, s=b0[:, :], d=sgcv(ic)[:, 0:512]:
                     E_.activation(d, s, AF.Silu), waits=[("pe", pe_c)])
            bfree(i0, "as_", a0)
            a1 = act(lambda E_, s=b1[:, :], d=sgcv(ic)[:, 512:1024]:
                     E_.activation(d, s, AF.Silu))
            bfree(i1, "as_", a1)
            return a1

        g_as = gu_phase("g", g_out)

        def u_out(ic, i0, i1, b0, b1, pe_c):
            m1 = dve(lambda E_, ic=ic: E_.tensor_tensor(SG_SCR[:, :], sgcv(ic), CB[:, :],
                                                        ALU.mult),
                     waits=[("as_", g_as), ("ds", cb_ds)])
            h0 = dve(lambda E_, ic=ic, s=b0[:, :]:
                     E_.tensor_tensor(HB[:, ic * TF:ic * TF + 512], SG_SCR[:, 0:512], s,
                                      ALU.mult),
                     waits=[("pe", pe_c), ("ds", m1)])
            bfree(i0, "ds", h0)
            h1 = dve(lambda E_, ic=ic, s=b1[:, :]:
                     E_.tensor_tensor(HB[:, ic * TF + 512:(ic + 1) * TF],
                                      SG_SCR[:, 512:1024], s, ALU.mult))
            bfree(i1, "ds", h1)
            return h1

        u_ds = gu_phase("u", u_out)
        xn_free_pe = pl.cnt("pe")

        last_acc = None
        for hc2 in range(NCH):
            qi = hc2 * 128 // 512
            w, gw, j, c0 = wv("d", qi)
            hcol = hc2 * 128 - c0
            i0, i1 = (4, 5) if hc2 % 2 == 0 else (6, 0)
            b0, b1 = P[i0], P[i1]
            lro = []
            for ic in range(NIC):
                l = w[:, ic, hcol:hcol + 128]
                lro.append((l, HB[:, ic * TF:ic * TF + 512], b0[:, :],
                            ic == 0, ic == NIC - 1))
                lro.append((l, HB[:, ic * TF + 512:(ic + 1) * TF], b1[:, :],
                            ic == 0, ic == NIC - 1))
            pe_c = mmg(lro, waits=[gw, ("ds", u_ds)], banks=(i0, i1))
            if hc2 == NCH - 1 or (hc2 + 1) * 128 >= c0 + 512:
                wq_done(j)
            v0 = dve(lambda E_, hc2=hc2, s=b0[:, :]:
                     E_.tensor_tensor(accv[:, hc2, 0:512], accv[:, hc2, 0:512], s,
                                      ALU.add),
                     waits=[("pe", pe_c)])
            bfree(i0, "ds", v0)
            last_acc = dve(lambda E_, hc2=hc2, s=b1[:, :]:
                           E_.tensor_tensor(accv[:, hc2, 512:1024],
                                            accv[:, hc2, 512:1024], s, ALU.add))
            bfree(i1, "ds", last_acc)
        return last_acc, xn_free_pe

    def emit_ffn_stage(xd, od, dep=()):
        s1 = None
        xn_free = None
        acc_free = ()
        for hf in range(2):
            tg0 = hf * TF
            xn_ds = emit_ln(xd, tg0, 0, dep, xn_free_pe=xn_free, acc_free=acc_free)
            gate_ds, sg_ds = emit_gating(xn_ds)
            pidx = [declare_expert_pieces(e) for e in range(12)]
            last = None
            for e in range(12):
                cb_ds = emit_cb(e, gate_ds if e < E8 else sg_ds)
                last, xn_free = emit_expert(e, xn_ds, cb_ds, pidx[e])
            s1 = S_AS.dma(fmv(od)[:, :, tg0:tg0 + TF], accv[:, :, :],
                          waits=[("ds", last)])
            acc_free = [s1]
        return s1

    def emit_attention(x1_dep):
        emit_ln(x1_d, 0, 0, x1_dep)
        xn2_ds = emit_ln(x1_d, TF, TF, x1_dep)

        BV = HB[:, 2048:4096]
        g_bv = S_G.dma(BV[0:1, :], bqkv_row_d.ap()[:, 2 * H:3 * H],
                       waits=[("ds", pl.cnt("ds"))])
        lro = [(ONESR[:, :], BV[0:1, b * 512:(b + 1) * 512], P[b][:, :], True, True)
               for b in range(4)]
        pebv = mmg(lro, waits=[g_bv] + CONST_W, banks=(0, 1, 2, 3))
        dbv = None
        for b in range(4):
            dbv = dve(lambda E_, b=b, s=P[b][:, :]:
                      E_.tensor_copy(BV[:, b * 512:(b + 1) * 512], s),
                      waits=[("pe", pebv)])
            bfree(b, "ds", dbv)

        wqv = wqkvT_d.ap().rearrange("(c p) o -> p c o", p=128)
        slotA_flat = ACCbf[:, 0:16384]
        slotB_flat = WS[:, 0:16384]
        QT = HB[:, 0:2048]
        qt_store = ("cs", 0)
        vt_store = {0: ("cs", 0), 1: ("cs", 0)}
        pe0 = pl.cnt("pe")
        slot_done = {"A": pe0, "B": pe0}
        cc_issued = {}

        def qk_load(pi, sl):
            flat = slotA_flat if sl == "A" else slotB_flat
            w = [("pe", slot_done[sl]), ("ds", xn2_ds)]
            if sl == "A":
                w.append(("ds", pl.cnt("ds")))
            return S_QW.dma(flat, wqv[:, :, pi * 1024:(pi + 1) * 1024], waits=w)

        def qk_piece(pi, sl, gq_w=None):
            flat = slotA_flat if sl == "A" else slotB_flat
            nonlocal qt_store
            if gq_w is None:
                gq_w = qk_load(pi, sl)
            slot = flat.rearrange("p (c o) -> p c o", c=NCH)
            for oi in range(8):
                oc = pi * 8 + oi
                lro = []
                for hc in range(NCH):
                    for tb in range(4):
                        lro.append((slot[:, hc, oi * 128:(oi + 1) * 128],
                                    xnv[:, hc, tb * 512:(tb + 1) * 512],
                                    P[tb][:, :], hc == 0, hc == NCH - 1))
                pe_c = mmg(lro, waits=[gq_w, ("ds", xn2_ds)], banks=(0, 1, 2, 3))
                alast = None
                for tb in range(4):
                    alast = act(lambda E_, tb=tb, oc=oc, s=P[tb][:, :]:
                                E_.activation(QT[:, tb * 512:(tb + 1) * 512], s,
                                              AF.Identity, bias=BQKV[:, oc:oc + 1]),
                                waits=[("pe", pe_c), qt_store] + CONST_W)
                    bfree(tb, "as_", alast)
                dst = q_dram if oc < 16 else cc_in_d
                row0 = (oc % 16) * 128
                qt_store = S_KS.dma(dst.ap()[row0:row0 + 128, :], QT[:, :],
                                    waits=[("as_", alast)])
            slot_done[sl] = pl.cnt("pe")

        def v_piece(pi, gq_w=None):
            if gq_w is None:
                gq_w = qk_load(pi, "B")
            slot = slotB_flat.rearrange("p (c o) -> p c o", c=NCH)
            VT = HB[:, 4096 + (pi - 4) * 1024: 4096 + (pi - 3) * 1024]
            for tc in range(NCH):
                pe_c = None
                for ob in range(2):
                    lro = []
                    for hc in range(NCH):
                        lro.append((xnv[:, hc, tc * 128:(tc + 1) * 128],
                                    slot[:, hc, ob * 512:(ob + 1) * 512],
                                    P[4 + ob][:, :], hc == 0, hc == NCH - 1))
                    pe_c = mmg(lro, waits=[gq_w, ("ds", xn2_ds)], banks=(4 + ob,))
                dl = None
                for ob in range(2):
                    c0_ = dve(lambda E_, ob=ob, s=P[4 + ob][:, :], vt=VT:
                              E_.tensor_copy(vt[:, ob * 512:(ob + 1) * 512], s),
                              waits=[("pe", pe_c), vt_store[pi - 4]])
                    bfree(4 + ob, "ds", c0_)
                    dl = dve(lambda E_, ob=ob, pi=pi, vt=VT:
                             E_.tensor_tensor(vt[:, ob * 512:(ob + 1) * 512],
                                              vt[:, ob * 512:(ob + 1) * 512],
                                              BV[:, (pi - 4) * 1024 + ob * 512:
                                                 (pi - 4) * 1024 + (ob + 1) * 512],
                                              ALU.add),
                             waits=[("ds", c0_), ("ds", dbv)])
                vt_store[pi - 4] = S_VS.dma(
                    cc_in_d.ap()[T + tc * 128:T + (tc + 1) * 128,
                                 (pi - 4) * 1024:(pi - 3) * 1024],
                    VT[:, :], waits=[("ds", dl)])
            slot_done["B"] = pl.cnt("pe")

        qk_piece(2, "A")
        qk_piece(3, "B")
        v_piece(4)
        gw5 = qk_load(5, "B")
        gw0 = qk_load(0, "A")
        v_piece(5, gw5)
        gw1 = qk_load(1, "B")

        kv_waits = S_KS.all_waits() + S_VS.all_waits()
        groups = ([[2 * g, 2 * g + 1] for g in range(world // 2)]
                  if world == 8 else [list(range(world))])
        pl.op("gpsimd",
              lambda E_: E_.collective_compute(
                  "AllGather", ALU.bypass,
                  replica_groups=groups,
                  ins=[cc_in_d.ap()], outs=[cc_out_d.ap()]),
              waits=kv_waits, inc="cs", inc_by=1)
        ccw = ("cs", pl.cnt("cs"))

        qk_piece(0, "A", gw0)
        qk_piece(1, "B", gw1)

        QTILE = ACCbf[:, 0:8192].rearrange("p (c t) -> p c t", c=NCH)
        SCq = [ACCbf[:, 8192 + q * 4096:8192 + (q + 1) * 4096] for q in range(4)]
        X1C = [ACC[:, 12288 + i * 512:12288 + (i + 1) * 512] for i in range(2)]
        PR = XN[:, 0:4096]
        PRT = XN[:, 4096:20480].rearrange("p (k t) -> p k t", k=32)
        CTX = XN[:, 20480:28672].rearrange("p (c t) -> p c t", c=NCH)
        KT = [WS[:, 0:8192], WS[:, 8192:16384]]
        VT4 = [HB[:, 4096 + i * 3584:4096 + (i + 1) * 3584] for i in range(2)]

        qv = q_dram.ap().rearrange("(c p) t -> p c t", p=128)
        kv_half = [cc_out_d.ap()[s * 2 * T:s * 2 * T + T, :]
                   .rearrange("(c p) t -> p c t", p=128) for s in range(2)]
        x1v = fmv(x1_d)
        x2v = fmv(x2_d)
        x2_store = {}
        last_x2 = None
        pe_qkv = pl.cnt("pe")
        kt_done = {0: pe_qkv, 1: pe_qkv}
        vt4_done = {0: pe_qkv, 1: pe_qkv}
        scq_free = {qt: ("pe", pe_qkv) for qt in range(4)}
        qtile_done = pe_qkv
        pr_done = 0
        dt_free = {0: ("cs", 0), 1: ("cs", 0)}

        for qb in range(4):
            gq_q = S_Q.dma(ACCbf[:, 0:8192], qv[:, :, qb * 512:(qb + 1) * 512],
                           waits=[("pe", qtile_done)] +
                                 (S_KS.all_waits() if qb == 0 else []))
            for kb in range(8):
                src2 = kv_half[kb // 4]
                kw = [("pe", kt_done[kb % 2]), ccw]
                gk = S_K.dma(KT[kb % 2],
                             src2[:, :, (kb % 4) * 512:((kb % 4) + 1) * 512], waits=kw)
                ktv = KT[kb % 2].rearrange("p (c t) -> p c t", c=NCH)
                for qt in range(4):
                    bi = qt if kb % 2 == 0 else (qt + 4) % 7
                    lro = []
                    for hc in range(NCH):
                        lro.append((QTILE[:, hc, qt * 128:(qt + 1) * 128],
                                    ktv[:, hc, :], P[bi][:, :], hc == 0, hc == NCH - 1))
                    pe_c = mmg(lro, waits=[gk, gq_q], banks=(bi,))
                    cw = [("pe", pe_c)]
                    if kb == 0:
                        cw.append(scq_free[qt])
                    cc2 = dve(lambda E_, s=P[bi][:, :], qt=qt, kb=kb:
                              E_.tensor_copy(SCq[qt][:, kb * 512:(kb + 1) * 512], s),
                              waits=cw)
                    bfree(bi, "ds", cc2)
                kt_done[kb % 2] = pl.cnt("pe")
            qtile_done = pl.cnt("pe")
            for qt in range(4):
                d1 = dve(lambda E_, qt=qt: E_.reduce_max(COL[5], SCq[qt], axis=AX.X,
                                                         negate=True))
                d2 = dve(lambda E_: E_.tensor_scalar(COL[6], COL[5], ISCL, None, ALU.mult),
                         waits=[("ds", d1)])
                a1 = act(lambda E_, qt=qt: E_.activation(PR[:, :], SCq[qt], AF.Exp,
                                                         bias=COL[6], scale=ISCL,
                                                         accum_out=COL[7]),
                         waits=[("ds", d2), ("pe", pr_done)])
                scq_free[qt] = ("as_", a1)
                d3 = dve(lambda E_: E_.reciprocal(COL[5], COL[7]), waits=[("as_", a1)])
                d4 = dve(lambda E_: E_.tensor_scalar(PR[:, :], PR[:, :], COL[5], None,
                                                     ALU.mult),
                         waits=[("ds", d3)])
                for kg in range(4):
                    pe_t = None
                    for kk in range(8):
                        ktc = kg * 8 + kk
                        pe_t = tr(PT[:, kk * 128:(kk + 1) * 128],
                                  PR[:, ktc * 128:(ktc + 1) * 128],
                                  waits=[("ds", d4)], banks=(7,) if kk == 0 else ())
                    prc = dve(lambda E_, kg=kg, qt=qt,
                              s=PT[:, :].rearrange("p (k t) -> p k t", k=8):
                              E_.tensor_copy(PRT[:, kg * 8:(kg + 1) * 8,
                                                 qt * 128:(qt + 1) * 128], s),
                              waits=[("pe", pe_t)])
                    bfree(7, "ds", prc)
                pr_done = pl.cnt("pe")
            prt_ds = pl.cnt("ds")
            ctx_as = None
            for gg, (h0, hn) in enumerate(((0, 7), (7, 7), (14, 2))):
                pe_c = None
                for kt4 in range(8):
                    r0 = (kt4 // 4) * 2 * T + T + (kt4 % 4) * 512
                    vw = [("pe", vt4_done[kt4 % 2]), ccw]
                    gv = S_V.dma(VT4[kt4 % 2][:, 0:4 * hn * 128],
                                 cc_out_d.ap()[r0:r0 + 512, h0 * 128:(h0 + hn) * 128]
                                 .rearrange("(c p) h -> p c h", p=128),
                                 waits=vw)
                    vt = VT4[kt4 % 2][:, 0:4 * hn * 128].rearrange(
                        "p (c h) -> p c h", c=4)
                    for c4 in range(4):
                        ktc = kt4 * 4 + c4
                        lro = []
                        for hh in range(hn):
                            lro.append((vt[:, c4, hh * 128:(hh + 1) * 128],
                                        PRT[:, ktc, :], P[hh][:, :],
                                        ktc == 0, ktc == 31))
                        pe_c = mmg(lro, waits=[gv, ("ds", prt_ds)],
                                   banks=tuple(range(hn)) if ktc == 0 else ())
                    vt4_done[kt4 % 2] = pl.cnt("pe")
                for hh in range(hn):
                    ca = act(lambda E_, hh=hh, h0=h0, s=P[hh][:, :]:
                             E_.copy(CTX[:, h0 + hh, :], s),
                             waits=[("pe", pe_c)])
                    bfree(hh, "as_", ca)
                    ctx_as = ca
            for pi in range(4):
                gw_w = S_QW.dma(KT[pi % 2][:, 0:8192],
                                woutT_d.ap().rearrange("(c p) o -> p c o", p=128)
                                [:, :, pi * 512:(pi + 1) * 512],
                                waits=[("pe", kt_done[pi % 2])])
                wot = KT[pi % 2].rearrange("p (c o) -> p c o", c=NCH)
                for oi in range(4):
                    oc = pi * 4 + oi
                    bi = oc % 2
                    lro = []
                    for hc in range(NCH):
                        lro.append((wot[:, hc, oi * 128:(oi + 1) * 128],
                                    CTX[:, hc, :], P[bi][:, :], hc == 0, hc == NCH - 1))
                    pe_c = mmg(lro, waits=[gw_w, ("as_", ctx_as)], banks=(bi,))
                    dtt = DT0 if bi == 0 else DT1
                    ao = act(lambda E_, s=P[bi][:, :], oc=oc, d=dtt:
                             E_.activation(d, s, AF.Identity, bias=BOUT[:, oc:oc + 1]),
                             waits=[("pe", pe_c), dt_free[bi]] + CONST_W)
                    bfree(bi, "as_", ao)
                    gx = S_XL.dma(X1C[oc % 2], x1v[:, oc, qb * 512:(qb + 1) * 512],
                                  waits=[x2_store.get(oc % 2, ("cs", 0))])
                    dx = dve(lambda E_, oc=oc, d=dtt: E_.tensor_tensor(
                        X1C[oc % 2], X1C[oc % 2], d, ALU.add),
                             waits=[("as_", ao), gx])
                    dt_free[bi] = ("ds", dx)
                    sx = S_XS.dma(x2v[:, oc, qb * 512:(qb + 1) * 512],
                                  X1C[oc % 2], waits=[("ds", dx)])
                    x2_store[oc % 2] = sx
                    last_x2 = sx
                kt_done[pi % 2] = pl.cnt("pe")
        return last_x2

    if upto == "ffn1":
        emit_ffn_stage(x0_d, y_d)
    elif upto == "attn":
        emit_ffn_stage(x0_d, x1_d)
        x1w = S_AS.all_waits()
        emit_attention(x1w)
        S_AS.dma(y_d.ap(), x2_d.ap(), waits=S_XS.all_waits())
    else:
        emit_ffn_stage(x0_d, x1_d)
        x1w = S_AS.all_waits()
        emit_attention(x1w)
        wq["floor"] = pl.cnt("pe")
        emit_ffn_stage(x2_d, y_d, dep=S_XS.all_waits())

    if dbg:
        if upto != "ffn1":
            dbg_cco = nc.dram_tensor("dbg_cco", [2 * 2 * T, H], BF16,
                                     kind="ExternalOutput")
            dbg_q = nc.dram_tensor("dbg_q", [H, T], BF16, kind="ExternalOutput")
            dbg_cci = nc.dram_tensor("dbg_cci", [2 * T, H], BF16, kind="ExternalOutput")
            dwx = [("pe", pl.cnt("pe")), ("ds", pl.cnt("ds")), ("as_", pl.cnt("as_"))]
            S_AS.dma(dbg_cco.ap(), cc_out_d.ap(), waits=dwx)
            S_AS.dma(dbg_q.ap(), q_dram.ap(), waits=dwx)
            S_AS.dma(dbg_cci.ap(), cc_in_d.ap(), waits=dwx)
        dbg_xn = nc.dram_tensor("dbg_xn", [128, NCH * TF], BF16, kind="ExternalOutput")
        dbg_cb = nc.dram_tensor("dbg_cb", [128, 1024], BF16, kind="ExternalOutput")
        dbg_hb = nc.dram_tensor("dbg_hb", [128, NIC * TF], BF16, kind="ExternalOutput")

        dw = [("pe", pl.cnt("pe")), ("ds", pl.cnt("ds")), ("as_", pl.cnt("as_"))]
        S_AS.dma(dbg_xn.ap().rearrange("p (c t) -> p c t", t=TF), xnv[:, :, 0:TF],
                 waits=dw)
        S_AS.dma(dbg_cb.ap(), CB[:, :], waits=dw)
        S_AS.dma(dbg_hb.ap(), HB[:, :], waits=dw)


    fin = S_AS.all_waits() + S_XS.all_waits()
    pl.op("scalar", lambda E_: E_.nop(), waits=fin)

    with nc.Block() as block:
        def mk(name):
            def _(E_):
                for fn, wl, inc, inc_by in pl.ops[name]:
                    for s, v in wl:
                        E_.wait_ge(sems[s], v)
                    r = fn(E_)
                    if inc is not None and r is not None:
                        r.then_inc(sems[inc], inc_by)
            return _
        block.sync(mk("sync"))
        block.gpsimd(mk("gpsimd"))
        block.tensor(mk("tensor"))
        block.vector(mk("vector"))
        block.scalar(mk("scalar"))
    ctx.close()
    return nc


_NC_CACHE = {}


def kernel(L0_attn_hidden_states, L0_attn_w_qkv, L0_attn_b_qkv, L0_attn_w_out,
           L0_attn_b_out, L0_ffn_gate_w, L0_ffn_w_gate, L0_ffn_w_up, L0_ffn_w_down,
           L0_ffn_sw_gate, L0_ffn_sw_up, L0_ffn_sw_down, L0_ffn_shared_gate_w):
    x0 = np.asarray(L0_attn_hidden_states, dtype=np.float32)
    w_qkv = np.asarray(L0_attn_w_qkv, np.float32)
    b_qkv = np.asarray(L0_attn_b_qkv, np.float32)
    w_out = np.asarray(L0_attn_w_out, np.float32)
    b_out = np.asarray(L0_attn_b_out, np.float32)
    gate_w = np.asarray(L0_ffn_gate_w, np.float32)

    shared = {
        "wqkvT": np.ascontiguousarray(w_qkv.T),
        "bqkv_pc": np.ascontiguousarray(b_qkv.reshape(48, 128).T),
        "bqkv_row": np.ascontiguousarray(b_qkv.reshape(1, 3 * H)),
        "woutT": np.ascontiguousarray(w_out.T),
        "bout_pc": np.ascontiguousarray(b_out.reshape(16, 128).T),
        "gatew_pc": np.ascontiguousarray(
            gate_w.T.reshape(16, 128, 8).transpose(1, 0, 2).reshape(128, 128)),
        "sgw_pc": np.ascontiguousarray(
            np.asarray(L0_ffn_shared_gate_w, np.float32).reshape(16, 128).T),
        "wg": np.ascontiguousarray(np.asarray(L0_ffn_w_gate, np.float32)),
        "wu": np.ascontiguousarray(np.asarray(L0_ffn_w_up, np.float32)),
        "wd": np.ascontiguousarray(np.asarray(L0_ffn_w_down, np.float32)),
        "swg": np.ascontiguousarray(np.asarray(L0_ffn_sw_gate, np.float32)),
        "swu": np.ascontiguousarray(np.asarray(L0_ffn_sw_up, np.float32)),
        "swd": np.ascontiguousarray(np.asarray(L0_ffn_sw_down, np.float32)),
    }
    in_maps = []
    for c in range(8):
        b, hf = c // 2, c % 2
        m = dict(shared)
        m["x0"] = np.ascontiguousarray(x0[b, hf * T:(hf + 1) * T, :].T)
        in_maps.append(m)

    if "nc" not in _NC_CACHE:
        _NC_CACHE["nc"] = build_program("full")
    res = run_bass_kernel_spmd(_NC_CACHE["nc"], in_maps, core_ids=list(range(8)))
    out = np.empty((4, 2 * T, H), np.float32)
    for c in range(8):
        b, hf = c // 2, c % 2
        out[b, hf * T:(hf + 1) * T, :] = np.asarray(res.results[c]["y"]).T
    return out
